# version 01 (final)
# baseline (speedup 1.0000x reference)
import numpy as np
from contextlib import ExitStack

import concourse.bass as bass
import concourse.mybir as mybir
from concourse.bass_utils import run_bass_kernel_spmd

F32 = mybir.dt.float32
BF16 = mybir.dt.bfloat16
I32 = mybir.dt.int32
AF = mybir.ActivationFunctionType
ALU = mybir.AluOpType

B, T, D = 16, 2048, 1024
NCORE = 8
BPC = B // NCORE
NTOK = BPC * T
NCH = D // 128
NTT = NTOK // 128
TCH = 512
NH, DK = 16, 64
NE0 = 15
FF1 = 4096
NE1, NS1 = 4, 2

SM_SHIFT = -40.0
SKIP_RAGGED = False
L1_TCMAX = 32
L1_PARTS = 7
L1_PHASEA = 3
L1_PRON = 29


class Ctr:
    __slots__ = ("h", "n")

    def __init__(self, h):
        self.h = h
        self.n = 0

    def inc(self, inst, k=1):
        inst.then_inc(self.h, k)
        self.n += k
        return self.n


def quake_rsqrt(v, ve, out_col, in_col, tmp_col, n_iter=3):
    def step(inst):
        ve.inc(inst)
        v.wait_ge(ve.h, ve.n)
    step(v.tensor_scalar(out=out_col.bitcast(I32), in0=in_col.bitcast(I32), scalar1=1,
                         scalar2=None, op0=ALU.logical_shift_right))
    step(v.tensor_scalar(out=out_col.bitcast(I32), in0=out_col.bitcast(I32), scalar1=-1.0,
                         scalar2=float(0x5F3759DF), op0=ALU.mult, op1=ALU.add))
    for _ in range(n_iter):
        step(v.tensor_tensor(out=tmp_col, in0=out_col, in1=out_col, op=ALU.mult))
        step(v.scalar_tensor_tensor(out=tmp_col, in0=in_col, scalar=-0.5, in1=tmp_col,
                                    op0=ALU.mult, op1=ALU.mult))
        step(v.scalar_tensor_tensor(out=out_col, in0=tmp_col, scalar=1.5, in1=out_col,
                                    op0=ALU.add, op1=ALU.mult))


def build_program(dbg=()):
    nc = bass.Bass("TRN2", target_bir_lowering=False, num_devices=8)
    dt = {}

    def din(name, shape, dtype=F32):
        dt[name] = nc.dram_tensor(name, list(shape), dtype, kind="ExternalInput")

    def dint(name, shape, dtype=F32, **kw):
        dt[name] = nc.dram_tensor(name, list(shape), dtype, **kw)

    def dout(name, shape, dtype=F32):
        dt[name] = nc.dram_tensor(name, list(shape), dtype, kind="ExternalOutput")

    din("x_fm", (D, NTOK))
    din("wqT", (D, D)); din("wkT", (D, D)); din("wvT", (D, D)); din("woT", (D, D))
    din("bq", (D,)); din("bk", (D,)); din("bv", (D,)); din("bo", (D,))
    din("embKT", (DK, 129)); din("embV", (129, DK))
    din("r0_W", (D, NE0)); din("r0_b", (NE0,))
    din("sW1", (D, D)); din("sb1", (D,)); din("sW2", (D, D)); din("sb2", (D,))
    din("eW1", (NE0, D, D)); din("eb1", (NE0, D))
    din("eW2", (NE0, D, D)); din("eb2", (NE0, D))
    din("miu_x", (D,)); din("lam", (5, D))
    din("A1", (D, 160)); din("Bp", (5, 32, D))
    din("td_miu", (D,)); din("tdA", (D, 64)); din("tdB", (64, D))
    din("u1", (D,))
    din("l1wkT", (D, D)); din("l1wvT", (D, D)); din("l1wrT", (D, D)); din("l1woT", (D, D))
    din("Wg1", (D, 160)); din("Wg2", (160, D))
    din("gamma", (D,)); din("beta", (D,))
    din("g1_w", (D, NE1)); din("g1_b", (NE1,))
    din("fsw1", (NS1, D, FF1)); din("fsb1", (NS1, FF1))
    din("fsw2", (NS1, FF1, D)); din("fsb2", (NS1, D))
    din("frw1", (NE1, D, FF1)); din("frb1", (NE1, FF1))
    din("frw2", (NE1, FF1, D)); din("frb2", (NE1, D))
    din("eb1T", (D, 16)); din("eb2m", (16, D)); 
    din("fb1T", (FF1, 6)); din("fb2m", (6, D))
    din("identity", (128, 128), BF16)
    din("embKT2", (128, 129))
    din("qkvo_bias", (4, D))
    din("rmask", (2, 128, 128), BF16)
    din("bwmask", (2, 128, 127), BF16)
    din("maskT", (128, 128), BF16)
    din("Mkhat_c", (128, 128), BF16)
    din("Mkbrv_c", (128, 128), BF16)
    din("INDf", (16, 1024), BF16)
    din("l1rows", (2, 128))
    din("l1cons", (128, 4))

    dout("out_fm", (D, NTOK))

    dint("xn_dram", (D, NTOK), BF16)
    dint("x1_dram", (D, NTOK), BF16)
    dint("stat_dram", (8, 128), F32)
    dint("cw_dram", (NE0, NTOK), F32)
    dint("scr_qe", (128, 640), BF16)
    dint("scr_wn", (128, 640), BF16)
    dint("scr_dn", (1, NH * 128), BF16)
    dint("cc_in", (160, NTOK), BF16)
    dint("cc_out", (8 * 160, NTOK), BF16, addr_space="Shared")
    dint("hl_dram", (5, 32, 5, 820), BF16)

    for name in dbg:
        if name == "xn":
            dout("dbg_xn", (D, NTOK), BF16)
        elif name == "ffn0":
            dout("dbg_ffn0", (D, NTOK), BF16)
        elif name == "l0sum":
            dout("dbg_l0sum", (D, NTOK), BF16)
        elif name == "fdbg":
            dout("dbg_eqk", (128, 1024), BF16)
            dout("dbg_qe0", (1, 256), BF16)
            dout("dbg_rows", (66, 128), BF16)
            dout("dbg_wnt", (128, 384), BF16)
        elif name == "pdbg":
            dout("dbg_pv", (66, 128), BF16)
            dout("dbg_pv2", (66, 128), BF16)
        elif name == "rdbg":
            dout("dbg_recipbc", (128, 8, 128), BF16)
            dout("dbg_dn", (1, 2048), F32)
            dout("dbg_araw", (128, 8, 128), BF16)
        elif name == "ffn1":
            dout("dbg_ffn1", (D, NTOK), BF16)
        elif name == "x1":
            dout("dbg_x1", (D, NTOK), BF16)
            dout("dbg_xn2", (D, NTOK), BF16)
        elif name == "awrk":
            dout("dbg_wrk", (128, 8192), BF16)
            dout("dbg_wrkf", (128, 2048), F32)
            dout("dbg_va", (128, 16 * 16 * 66), BF16)
            dout("dbg_kb", (128, 8, 2048), BF16)
            dout("dbg_evp", (127, 64), BF16)
            dout("dbg_evrow", (66, 192), BF16)
        elif name == "cw":
            dout("dbg_cw", (NE0, NTOK), F32)
        elif name == "l1sum":
            dout("dbg_l1sum", (D, NTOK), BF16)
        elif name == "l1w":
            dout("dbg_wrk1", (128, 8192), BF16)
            dout("dbg_wrkf1", (128, 2048), F32)
            dout("dbg_vb1", (128, 8192), BF16)
            dout("dbg_hl", (5, 32, 5, 820), BF16)
            dout("dbg_cc", (160, NTOK), BF16)

    with ExitStack() as ctx:
        sb = {}

        def sbuf(name, shape, dtype):
            sb[name] = ctx.enter_context(nc.sbuf_tensor(name, list(shape), dtype))

        for i in range(8):
            sb[f"PB{i}"] = ctx.enter_context(nc.psum_tensor(f"PB{i}", [128, 512], F32))

        sbuf("ACC2", (128, NCH, NTOK), BF16)
        sbuf("XNS", (128, NCH, T), BF16)
        sbuf("KBUF", (128, NCH, T), BF16)
        sbuf("VBUF", (128, NH * 66 * (T // 128)), BF16)
        sbuf("WRK", (128, 8192), BF16)
        sbuf("WRKF", (128, 2048), F32)
        sbuf("onesb", (128, 512), BF16)
        sbuf("rowsb", (1, 64), BF16)
        sbuf("eb2msb", (16, D), BF16)
        sbuf("cwxsb", (16, 512), BF16)
        sbuf("eb1sb", (128, 8, 16), F32)
        sbuf("r0Wsb", (128, 8, NE0), BF16)
        sbuf("embKTsb", (128, 129), BF16)
        sbuf("embVpsb", (127, 64), BF16)
        sbuf("embVrow", (66, 192), BF16)
        sbuf("qkvob", (1, 4 * D), BF16)
        sbuf("rmasksb", (128, 256), BF16)
        sbuf("bwmasksb", (128, 254), BF16)
        sbuf("recipbc", (128, 8, 128), BF16)
        sbuf("shiftcol", (128, 1), F32)
        sbuf("z1col", (128, 2), BF16)
        sbuf("zrow", (1, 128), BF16)
        sbuf("qef32", (128, 2), F32)
        sbuf("cwbc1", (128, 512), BF16)
        sbuf("fb1sb", (128, 32, 6), F32)
        sbuf("identsb", (128, 128), BF16)
        sbuf("identf", (128, 128), F32)

        sems = {}
        for s in ["dma0", "dma1", "dma2", "te", "ve", "se", "ge", "ph"]:
            h = ctx.enter_context(nc.semaphore(s))
            sems[s] = Ctr(h)

        block = ctx.enter_context(nc.Block())
        E = Env(nc, block, dt, sb, sems, dbg)
        emit_prologue(E)
        emit_ln(E, src="x_fm", src_f32=True, dst="xn_dram", dbg_tap="xn" in dbg)
        E.xn_written = E.sems["dma0"].n
        emit_l0ffn(E)
        emit_l0attn(E)
        emit_resln(E)
        emit_l1ffn(E)
        emit_l1attn(E)
        emit_final(E)
        emit_finish(E)

    return nc


class Env:
    def __init__(self, nc, block, dt, sb, sems, dbg):
        self.nc, self.block, self.dt, self.sb, self.sems, self.dbg = nc, block, dt, sb, sems, dbg

    def pb(self, i, lo=0, hi=512):
        return self.sb[f"PB{i}"][:, lo:hi]


def emit_prologue(E):
    sb, dt, sems = E.sb, E.dt, E.sems

    @E.block.sync
    def _(sy):
        sy.dma_start(out=sb["identsb"][:, :], in_=dt["identity"].ap()).then_inc(sems["dma0"].h, 16)
    sems["dma0"].n += 16

    @E.block.gpsimd
    def _(g):
        g.memset(sb["onesb"][:, :], 1.0)
        g.engine_nop().then_inc(sems["ge"].h, 1)
    sems["ge"].n += 1

    @E.block.vector
    def _(v):
        v.wait_ge(sems["dma0"].h, sems["dma0"].n)
        v.tensor_copy(sb["identf"][:, :], sb["identsb"][:, :]).then_inc(sems["ve"].h, 1)
    sems["ve"].n += 1
    E.ones_ready = sems["ge"].n
    E.ident_ready = sems["dma0"].n


def emit_ln(E, src, src_f32, dst, dbg_tap=False):
    nc, block, dt, sb, sems = E.nc, E.block, E.dt, E.sb, E.sems
    src_ap = dt[src].ap().rearrange("(c p) n -> p c n", p=128)
    dst_ap = dt[dst].ap().rearrange("(c p) n -> p c n", p=128)
    dma0, dma1, dma2 = sems["dma0"], sems["dma1"], sems["dma2"]
    te, ve, se, ge = sems["te"], sems["ve"], sems["se"], sems["ge"]
    ident, onesb = sb["identsb"], sb["onesb"]
    xb = sb["WRK"].rearrange("p (c n) -> p c n", c=16)
    strow = sb["WRKF"][:, 0:16]
    stT = sb["WRKF"][:, 128:256].rearrange("p (a b) -> p a b", a=1)
    stT8 = sb["WRKF"][0:8, 256:384]
    bc_m = sb["WRKF"][:, 512:576].bitcast(BF16)
    bc_r = sb["WRKF"][:, 576:640].bitcast(BF16)

    NCHK = NTOK // TCH
    for ck in range(NCHK):
        t0 = ck * TCH

        @block.gpsimd
        def _(g, ck=ck, t0=t0):
            g.wait_ge(te.h, te.n)
            g.wait_ge(se.h, se.n)
            g.wait_ge(ve.h, ve.n)
            g.wait_ge(dma0.h, dma0.n)
            if src_f32:
                g.dma_start(out=xb[:, 0:8, :], in_=src_ap[:, :, t0:t0 + TCH]).then_inc(dma1.h, 16)
            else:
                g.dma_start(out=xb[:, 0:8, :], in_=src_ap[:, :, t0:t0 + TCH]).then_inc(dma1.h, 16)
        dma1.n += 16
        ld = dma1.n

        @block.scalar
        def _(s, ld=ld):
            s.wait_ge(dma1.h, ld)
            inst = None
            for c in range(NCH):
                inst = s.activation(out=xb[:, 8 + c, :], in_=xb[:, c, :], func=AF.Square)
            inst.then_inc(se.h, 1)
        se.n += 1
        sq_done = se.n

        @block.tensor
        def _(t, ld=ld, sq_done=sq_done):
            t.wait_ge(dma1.h, ld)
            t.wait_ge(ge.h, E.ones_ready)
            t.wait_ge(ve.h, ve.n)
            inst = None
            for tt in range(4):
                for c in range(NCH):
                    inst = t.matmul(E.pb(0, tt, tt + 1),
                                    lhsT=xb[:, c, tt * 128:(tt + 1) * 128],
                                    rhs=onesb[:, 0:1],
                                    start=(c == 0), stop=(c == NCH - 1))
            t.wait_ge(se.h, sq_done)
            for tt in range(4):
                for c in range(NCH):
                    inst = t.matmul(E.pb(0, 8 + tt, 9 + tt),
                                    lhsT=xb[:, 8 + c, tt * 128:(tt + 1) * 128],
                                    rhs=onesb[:, 0:1],
                                    start=(c == 0), stop=(c == NCH - 1))
            inst.then_inc(te.h, 1)
        te.n += 1
        mm_stats = te.n

        @block.vector
        def _(v, mm_stats=mm_stats):
            v.wait_ge(te.h, mm_stats)

            def step(inst):
                ve.inc(inst)
                v.wait_ge(ve.h, ve.n)
            step(v.tensor_scalar(out=strow[:, 0:4], in0=E.pb(0, 0, 4), scalar1=1.0 / D,
                                 scalar2=None, op0=ALU.mult))
            step(v.tensor_scalar(out=strow[:, 8:12], in0=E.pb(0, 8, 12), scalar1=1.0 / D,
                                 scalar2=None, op0=ALU.mult))
            step(v.tensor_tensor(out=strow[:, 12:16], in0=strow[:, 0:4], in1=strow[:, 0:4],
                                 op=ALU.mult))
            step(v.scalar_tensor_tensor(out=strow[:, 8:12], in0=strow[:, 12:16], scalar=-1.0,
                                        in1=strow[:, 8:12], op0=ALU.mult, op1=ALU.add))
            step(v.tensor_scalar(out=strow[:, 8:12], in0=strow[:, 8:12], scalar1=1e-5,
                                 scalar2=None, op0=ALU.add))
            quake_rsqrt(v, ve, strow[:, 4:8], strow[:, 8:12], strow[:, 12:16])
        stats_done = ve.n

        @block.tensor
        def _(t, stats_done=stats_done):
            t.wait_ge(ve.h, stats_done)
            t.wait_ge(dma0.h, E.ident_ready)
            t.transpose(E.pb(1, 0, 128)[0:8, :], in_=strow[:, 0:8],
                        identity=sb["identf"][:, :]).then_inc(te.h, 1)
        te.n += 1

        @block.vector
        def _(v):
            v.wait_ge(te.h, te.n)
            v.tensor_copy(stT8[:, :], E.pb(1, 0, 128)[0:8, :]).then_inc(ve.h, 1)
        ve.n += 1

        @block.sync
        def _(sy):
            sy.wait_ge(ve.h, ve.n)
            sy.dma_start(out=dt["stat_dram"].ap()[:, :], in_=stT8[:, :]).then_inc(dma2.h, 16)
        dma2.n += 16
        stat_w = dma2.n

        for tt in range(4):
            @block.gpsimd
            def _(g, tt=tt, stat_w=stat_w):
                g.wait_ge(dma2.h, stat_w)
                if tt > 0:
                    g.wait_ge(ve.h, ve.n)
                sd = dt["stat_dram"].ap()
                mrow = bass.AP(tensor=sd.tensor, offset=tt * 128, ap=[[0, 128], [1, 128]])
                rrow = bass.AP(tensor=sd.tensor, offset=(4 + tt) * 128, ap=[[0, 128], [1, 128]])
                g.dma_start(out=bc_m, in_=mrow).then_inc(dma2.h, 16)
                g.dma_start(out=bc_r, in_=rrow).then_inc(dma2.h, 16)
            dma2.n += 32
            bc_ld = dma2.n

            @block.vector
            def _(v, tt=tt, bc_ld=bc_ld):
                v.wait_ge(dma2.h, bc_ld)

                def step(inst):
                    ve.inc(inst)
                v.wait_ge(ve.h, ve.n)
                for c in range(NCH):
                    sl = xb[:, c, tt * 128:(tt + 1) * 128]
                    step(v.tensor_tensor(out=sl, in0=sl, in1=bc_m, op=ALU.subtract))
                v.wait_ge(ve.h, ve.n)
                for c in range(NCH):
                    sl = xb[:, c, tt * 128:(tt + 1) * 128]
                    step(v.tensor_tensor(out=sl, in0=sl, in1=bc_r, op=ALU.mult))
                v.wait_ge(ve.h, ve.n)
                nc.vector.engine_nop().then_inc(ve.h, 1)
            ve.n += 1

        xn_done = ve.n

        @block.sync
        def _(sy, xn_done=xn_done, t0=t0):
            sy.wait_ge(ve.h, xn_done)
            sy.dma_start(out=dst_ap[:, :, t0:t0 + TCH], in_=xb[:, 0:8, :]).then_inc(dma0.h, 16)
            if dbg_tap:
                xd = dt["dbg_xn"].ap().rearrange("(c p) n -> p c n", p=128)
                sy.dma_start(out=xd[:, :, t0:t0 + TCH], in_=xb[:, 0:8, :]).then_inc(dma0.h, 16)
        dma0.n += 16 + (16 if dbg_tap else 0)




def emit_l0ffn(E):
    nc, block, dt, sb, sems = E.nc, E.block, E.dt, E.sb, E.sems
    dma0, dma1, dma2 = sems["dma0"], sems["dma1"], sems["dma2"]
    te, ve, se, ge = sems["te"], sems["ve"], sems["se"], sems["ge"]
    onesb, identf = sb["onesb"], sb["identf"]
    xn_ap = dt["xn_dram"].ap().rearrange("(c p) n -> p c n", p=128)
    ACC2 = sb["ACC2"]

    VB = sb["VBUF"]
    xnbuf = [VB[:, 0:4096].rearrange("p (c n) -> p c n", c=8),
             VB[:, 4096:8192].rearrange("p (c n) -> p c n", c=8)]
    h_sb = VB[:, 8192:12288].rearrange("p (c n) -> p c n", c=8)
    cwbc = VB[:, 12288:12800]
    wbuf = [sb["KBUF"].rearrange("p c n -> p (c n)"), sb["XNS"].rearrange("p c n -> p (c n)")]
    WF = sb["WRKF"]
    probs = WF[:, 0:15]
    scr1 = WF[:, 16:31]
    scr2 = WF[:, 32:47]
    cwf = WF[:, 64:79]
    cols = WF[:, 48:58]
    cwrow = WF[0:15, 96:224]
    rowsb = sb["rowsb"]

    @block.gpsimd
    def _(g):
        g.wait_ge(ve.h, ve.n)
        g.dma_start(out=sb["eb1sb"][:, :, :],
                    in_=dt["eb1T"].ap().rearrange("(c p) e -> p c e", p=128)).then_inc(dma1.h, 16)
        g.dma_start(out=sb["eb2msb"][:, :], in_=dt["eb2m"].ap()).then_inc(dma1.h, 16)
        g.dma_start(out=rowsb[0:1, 0:15], in_=dt["r0_b"].ap().unsqueeze(0)).then_inc(dma1.h, 16)
        g.memset(sb["cwxsb"][:, :], 1.0)
        g.dma_start(out=sb["r0Wsb"][:, :, :],
                    in_=dt["r0_W"].ap().rearrange("(c p) e -> p c e", p=128)).then_inc(dma1.h, 16)
    dma1.n += 64
    pro_ld = dma1.n

    for tt in range(NTT):
        t0 = tt * 128
        slot = tt % 2
        @block.sync
        def _(sy, tt=tt, t0=t0, slot=slot, tev=te.n):
            sy.wait_ge(dma0.h, E.xn_written)
            if tt >= 2:
                sy.wait_ge(te.h, tev)
            sy.dma_start(out=xnbuf[slot][:, :, 0:128], in_=xn_ap[:, :, t0:t0 + 128]).then_inc(dma0.h, 16)
        dma0.n += 16
        xld = dma0.n

        @block.tensor
        def _(t, xld=xld, slot=slot, tt=tt, vev=None, sev=None):
            t.wait_ge(dma0.h, xld)
            t.wait_ge(dma1.h, pro_ld)
            if tt > 0:
                t.wait_ge(ve.h, ve.n)
                t.wait_ge(se.h, se.n)
            inst = None
            for c in range(NCH):
                inst = t.matmul(E.pb(6, 0, 15), lhsT=xnbuf[slot][:, c, 0:128],
                                rhs=sb["r0Wsb"][:, c, :], start=(c == 0), stop=False)
            inst = t.matmul(E.pb(6, 0, 15), lhsT=onesb[0:1, 0:128],
                            rhs=rowsb[0:1, 0:15], start=False, stop=True)
            inst.then_inc(te.h, 1)
        te.n += 1
        lg_done = te.n

        @block.vector
        def _(v, lg_done=lg_done):
            v.wait_ge(te.h, lg_done)

            def step(inst):
                ve.inc(inst)
                v.wait_ge(ve.h, ve.n)
            step(v.tensor_scalar(out=scr1, in0=E.pb(6, 0, 15), scalar1=0.0, scalar2=-3e38,
                                 op0=ALU.add, op1=ALU.max, accum_out=cols[:, 0:1]))
            step(v.tensor_scalar(out=cols[:, 1:2], in0=cols[:, 0:1], scalar1=-1.0,
                                 scalar2=None, op0=ALU.mult))
        ve.n += 0

        @block.scalar
        def _(s):
            s.wait_ge(ve.h, ve.n)
            s.activation(out=probs, in_=E.pb(6, 0, 15), func=AF.Exp, bias=cols[:, 1:2],
                         scale=1.0, accum_out=cols[:, 2:3]).then_inc(se.h, 1)
        se.n += 1

        @block.vector
        def _(v):
            v.wait_ge(se.h, se.n)

            def step(inst):
                ve.inc(inst)
                v.wait_ge(ve.h, ve.n)
            step(v.reciprocal(out=cols[:, 3:4], in_=cols[:, 2:3]))
            step(v.tensor_scalar(out=probs, in0=probs, scalar1=cols[:, 3:4], scalar2=None,
                                 op0=ALU.mult))
            step(v.tensor_scalar(out=scr1, in0=probs, scalar1=0.0, scalar2=-3e38,
                                 op0=ALU.add, op1=ALU.max, accum_out=cols[:, 4:5]))
            step(v.scalar_tensor_tensor(out=scr2, in0=probs, scalar=cols[:, 4:5], in1=probs,
                                        op0=ALU.is_lt, op1=ALU.mult))
            step(v.tensor_scalar(out=scr1, in0=scr2, scalar1=0.0, scalar2=-3e38,
                                 op0=ALU.add, op1=ALU.max, accum_out=cols[:, 5:6]))
            step(v.scalar_tensor_tensor(out=scr2, in0=scr2, scalar=cols[:, 5:6], in1=scr2,
                                        op0=ALU.is_lt, op1=ALU.mult))
            step(v.tensor_scalar(out=scr1, in0=scr2, scalar1=0.0, scalar2=-3e38,
                                 op0=ALU.add, op1=ALU.max, accum_out=cols[:, 6:7]))
            step(v.scalar_tensor_tensor(out=cwf, in0=probs, scalar=cols[:, 6:7], in1=probs,
                                        op0=ALU.is_ge, op1=ALU.mult))
        cw_ready = ve.n

        @block.tensor
        def _(t, cw_ready=cw_ready, tt=tt):
            t.wait_ge(ve.h, cw_ready)
            t.transpose(E.pb(7, 0, 128)[0:15, :], in_=cwf, identity=identf[:, :]).then_inc(te.h, 1)
        te.n += 1

        @block.vector
        def _(v):
            v.wait_ge(te.h, te.n)
            v.tensor_copy(cwrow, E.pb(7, 0, 128)[0:15, :]).then_inc(ve.h, 1)
        ve.n += 1

        @block.sync
        def _(sy, tt=tt, t0=t0):
            sy.wait_ge(ve.h, ve.n)
            sy.dma_start(out=dt["cw_dram"].ap()[:, t0:t0 + 128], in_=cwrow).then_inc(dma2.h, 16)
        dma2.n += 16

    router_done_dma = dma2.n
    if "cw" in E.dbg:
        @block.sync
        def _(sy):
            sy.wait_ge(dma2.h, router_done_dma)
            sy.dma_start(out=dt["dbg_cw"].ap()[:, :], in_=dt["cw_dram"].ap()[:, :]).then_inc(dma2.h, 16)
        dma2.n += 16

    TC2 = 512
    NC2 = NTOK // TC2

    def wsrc(e_idx):
        if e_idx == 0:
            return dt["sW1"].ap(), dt["sW2"].ap()
        r = e_idx - 1
        return dt["eW1"].ap()[r], dt["eW2"].ap()[r]

    wload = {}

    def emit_wload(e_idx, after_te):
        buf = wbuf[e_idx % 2]

        @block.gpsimd
        def _(g, e_idx=e_idx, after_te=after_te, buf=buf):
            g.wait_ge(te.h, after_te)
            w1, w2 = wsrc(e_idx)
            g.dma_start(out=buf[:, 0:8192].rearrange("p (c n) -> p c n", c=8),
                        in_=w1.rearrange("(c p) n -> p c n", p=128)).then_inc(dma1.h, 16)
            g.dma_start(out=buf[:, 8192:16384].rearrange("p (c n) -> p c n", c=8),
                        in_=w2.rearrange("(c p) n -> p c n", p=128)).then_inc(dma1.h, 16)
        dma1.n += 32
        wload[e_idx] = dma1.n

    emit_wload(0, 0)
    emit_wload(1, 0)
    router_te_end = te.n

    relu_done = {}
    mm1_done = {}
    mm2_done = {}
    acc_done = {}
    acc_slice = {}
    xn_loaded = {}
    cw_loaded = {}
    mm2_all = {}
    last_mm2_all = [te.n]
    xn_slot_free = {0: te.n, 1: te.n}

    for e_idx in range(16):
        buf = wbuf[e_idx % 2]
        w1 = buf[:, 0:8192].rearrange("p (c n) -> p c n", c=8)
        w2 = buf[:, 8192:16384].rearrange("p (c n) -> p c n", c=8)
        routed = e_idx > 0
        r = e_idx - 1
        bias_col = 15 if e_idx == 0 else r
        for tc in range(NC2):
            t0 = tc * TC2
            slot = tc % 2
            @block.sync
            def _(sy, slot=slot, t0=t0, free_at=xn_slot_free[slot]):
                sy.wait_ge(te.h, free_at)
                sy.dma_start(out=xnbuf[slot][:, :, :],
                             in_=xn_ap[:, :, t0:t0 + TC2]).then_inc(dma0.h, 16)
            dma0.n += 16
            xn_loaded[(e_idx, tc)] = dma0.n

            if routed:
                @block.gpsimd
                def _(g, r=r, t0=t0, e_idx=e_idx, tc=tc, lm2=last_mm2_all[0]):
                    g.wait_ge(ge.h, ge.n)
                    g.wait_ge(te.h, lm2)
                    g.wait_ge(dma2.h, router_done_dma)
                    srcap = dt["cw_dram"].ap()
                    ap = bass.AP(tensor=srcap.tensor, offset=r * NTOK + t0, ap=[[0, 128], [1, TC2]])
                    g.dma_start(out=cwbc, in_=ap).then_inc(dma1.h, 16)
                dma1.n += 16
                cw_loaded[(e_idx, tc)] = dma1.n

            for hs in range(NCH):
                bank = 2 + (hs % 2)

                @block.tensor
                def _(t, e_idx=e_idx, tc=tc, hs=hs, bank=bank, slot=slot):
                    t.wait_ge(dma0.h, xn_loaded[(e_idx, tc)])
                    t.wait_ge(dma1.h, wload[e_idx])
                    t.wait_ge(se.h, relu_done.get((bank,), 0))
                    inst = None
                    for c in range(NCH):
                        inst = t.matmul(E.pb(bank), lhsT=w1[:, c, hs * 128:(hs + 1) * 128],
                                        rhs=xnbuf[slot][:, c, :], start=(c == 0), stop=(c == NCH - 1))
                    inst.then_inc(te.h, 1)
                te.n += 1
                mm1_done[(e_idx, tc, hs)] = te.n

                @block.scalar
                def _(s, e_idx=e_idx, tc=tc, hs=hs, bank=bank, bias_col=bias_col,
                      lm2=last_mm2_all[0], geval=ge.n):
                    s.wait_ge(te.h, mm1_done[(e_idx, tc, hs)])
                    s.wait_ge(ge.h, geval)
                    s.wait_ge(te.h, lm2)
                    s.activation(out=h_sb[:, hs, :], in_=E.pb(bank), func=AF.Relu,
                                 bias=sb["eb1sb"][:, hs, bias_col:bias_col + 1],
                                 scale=1.0).then_inc(se.h, 1)
                se.n += 1
                relu_done[(bank,)] = se.n
                relu_done[(e_idx, tc, hs)] = se.n

            mm1_done[(e_idx, tc)] = mm1_done[(e_idx, tc, NCH - 1)]
            xn_slot_free[slot] = mm1_done[(e_idx, tc)]

            if routed:
                @block.gpsimd
                def _(g, e_idx=e_idx, tc=tc):
                    g.wait_ge(dma1.h, cw_loaded[(e_idx, tc)])
                    inst = None
                    for hs in range(NCH):
                        g.wait_ge(se.h, relu_done[(e_idx, tc, hs)])
                        inst = g.tensor_tensor(out=h_sb[:, hs, :], in0=h_sb[:, hs, :],
                                               in1=cwbc, op=ALU.mult)
                    inst.then_inc(ge.h, 1)
                ge.n += 1
            h_ready_sem, h_ready_val = (ge, ge.n) if routed else (se, relu_done[(e_idx, tc, NCH - 1)])

            if e_idx == 0:
                @block.gpsimd
                def _(g, t0=t0, lm2=last_mm2_all[0]):
                    g.wait_ge(dma2.h, router_done_dma)
                    g.wait_ge(te.h, lm2)
                    g.dma_start(out=sb["cwxsb"][0:15, :],
                                in_=dt["cw_dram"].ap()[:, t0:t0 + TC2]).then_inc(dma1.h, 16)
                dma1.n += 16
                cw_loaded[(0, tc)] = dma1.n

            for ds in range(NCH):
                bank = ds % 2

                @block.tensor
                def _(t, e_idx=e_idx, tc=tc, ds=ds, bank=bank, routed=routed, r=r,
                      h_ready_sem=h_ready_sem, h_ready_val=h_ready_val):
                    t.wait_ge(h_ready_sem.h, h_ready_val)
                    t.wait_ge(ve.h, acc_done.get((bank,), 0))
                    inst = None
                    for hc in range(NCH):
                        inst = t.matmul(E.pb(bank), lhsT=w2[:, hc, ds * 128:(ds + 1) * 128],
                                        rhs=h_sb[:, hc, :], start=(hc == 0),
                                        stop=(routed and hc == NCH - 1))
                    if not routed:
                        t.wait_ge(dma1.h, cw_loaded[(0, tc)])
                        inst = t.matmul(E.pb(bank), lhsT=sb["eb2msb"][:, ds * 128:(ds + 1) * 128],
                                        rhs=sb["cwxsb"][:, :], start=False, stop=True)
                    inst.then_inc(te.h, 1)
                te.n += 1
                mm2_done[(e_idx, tc, ds)] = te.n

                @block.vector
                def _(v, e_idx=e_idx, tc=tc, ds=ds, bank=bank, t0=t0,
                      prev=None):
                    v.wait_ge(te.h, mm2_done[(e_idx, tc, ds)])
                    v.wait_ge(ve.h, acc_slice.get((tc, ds), 0))
                    dst = ACC2[:, ds, t0:t0 + TC2]
                    if e_idx == 0:
                        v.tensor_copy(dst, E.pb(bank)).then_inc(ve.h, 1)
                    else:
                        v.tensor_tensor(out=dst, in0=dst, in1=E.pb(bank),
                                        op=ALU.add).then_inc(ve.h, 1)
                ve.n += 1
                acc_done[(bank,)] = ve.n
                acc_slice[(tc, ds)] = ve.n
            mm2_all[(e_idx, tc)] = te.n
            last_mm2_all[0] = te.n
        if e_idx + 2 <= 15:
            emit_wload(e_idx + 2, te.n)

    if "ffn0" in E.dbg:
        dbg_ap = dt["dbg_ffn0"].ap().rearrange("(c p) n -> p c n", p=128)

        @block.sync
        def _(sy):
            sy.wait_ge(ve.h, ve.n)
            sy.dma_start(out=dbg_ap[:, :, :], in_=ACC2[:, :, :]).then_inc(dma2.h, 16)
        dma2.n += 16




def emit_l0attn(E):
    nc, block, dt, sb, sems = E.nc, E.block, E.dt, E.sb, E.sems
    dma0, dma1, dma2 = sems["dma0"], sems["dma1"], sems["dma2"]
    te, ve, se, ge = sems["te"], sems["ve"], sems["se"], sems["ge"]
    onesb = sb["onesb"]
    xn_ap = dt["xn_dram"].ap().rearrange("(c p) n -> p c n", p=128)
    ACC2 = sb["ACC2"]
    KB = sb["KBUF"]
    VA = sb["VBUF"][:, 0:16 * 16 * 66].rearrange("p (k h n) -> p k h n", k=16, h=16)
    WQO = sb["XNS"].rearrange("p c n -> p (c n)")
    wq = WQO[:, 0:8192].rearrange("p (c n) -> p c n", c=8)
    wo = WQO[:, 8192:16384].rearrange("p (c n) -> p c n", c=8)
    xn512 = WQO[:, 0:4096].rearrange("p (c n) -> p c n", c=8)
    WRK = sb["WRK"]
    wkv_w = WRK.rearrange("p (c n) -> p c n", c=8)
    qtile = WRK[:, 0:1024].rearrange("p (c n) -> p c n", c=8)
    attnraw = WRK[:, 1024:2048].rearrange("p (c n) -> p c n", c=8)
    qepad = WRK[:, 2048:2688]
    erel = WRK[:, 2688:3072]
    wnear = WRK[:, 3072:3456]
    wnearT = WRK[:, 3456:3840]
    eqkT = WRK[:, 3840:4864]
    bwt = WRK[:, 4864:4992]
    bwTs = WRK[:, 4992:5120]
    mskd = WRK[:, 5120:5248]
    rowbuf = WRK[64:66, 5248:5376]
    qe0row = WRK[0:1, 5376:5632]
    xnt = WRK[:, 5632:6656].rearrange("p (c n) -> p c n", c=8)
    dnrow = sb["WRKF"][64:65, 0:2048]
    PADL = 256
    PBbf = [E.sb[f"PB{i}"].bitcast(BF16) for i in range(8)]

    @block.gpsimd
    def _(g):
        g.wait_ge(ve.h, ve.n)
        g.memset(WRK[:, 0:640], 0.0)
        g.dma_start(out=dt["scr_wn"].ap()[:, :], in_=WRK[:, 0:640]).then_inc(dma1.h, 16)
        g.dma_start(out=sb["embKTsb"][:, :], in_=dt["embKT2"].ap()).then_inc(dma1.h, 16)
        g.dma_start(out=sb["qkvob"][0:1, :],
                    in_=dt["qkvo_bias"].ap().rearrange("a d -> (a d)").unsqueeze(0)).then_inc(dma1.h, 16)
        g.dma_start(out=sb["rmasksb"][:, 0:128], in_=dt["rmask"].ap()[0]).then_inc(dma1.h, 16)
        g.dma_start(out=sb["rmasksb"][:, 128:256], in_=dt["rmask"].ap()[1]).then_inc(dma1.h, 16)
        g.dma_start(out=sb["bwmasksb"][:, 0:127], in_=dt["bwmask"].ap()[0]).then_inc(dma1.h, 16)
        g.dma_start(out=sb["bwmasksb"][:, 127:254], in_=dt["bwmask"].ap()[1]).then_inc(dma1.h, 16)
        ev = dt["embV"].ap()
        g.dma_start(out=sb["embVpsb"][:, :], in_=ev[1:128, :]).then_inc(dma1.h, 16)
        bc0 = bass.AP(tensor=ev.tensor, offset=0, ap=[[0, 127], [1, 64]])
        g.dma_start(out=WRK[0:127, 0:64], in_=bc0).then_inc(dma1.h, 16)
        g.memset(sb["embVrow"][64:66, 64:128], 0.0)
        g.dma_start(out=sb["embVrow"][64:65, 0:64], in_=ev[0:1, :]).then_inc(dma1.h, 16)
        g.dma_start(out=sb["embVrow"][65:66, 0:64], in_=ev[128:129, :]).then_inc(dma1.h, 16)
        g.dma_start(out=sb["embVrow"][65:66, 64:128], in_=ev[0:1, :]).then_inc(dma1.h, 16)
        g.memset(VA[:, :, :, 64:66], 1.0)
        g.memset(sb["shiftcol"][:, :], SM_SHIFT)
        g.memset(sb["z1col"][:, :], 0.0)
        g.memset(sb["z1col"][:, 1:2], 1.0)
        g.memset(sb["zrow"][0:1, :], 0.0)
        g.engine_nop().then_inc(ge.h, 1)
    dma1.n += 192
    ge.n += 1

    @block.vector
    def _(v):
        v.wait_ge(dma1.h, dma1.n)
        v.wait_ge(ge.h, ge.n)
        v.tensor_tensor(out=sb["embVpsb"][:, :], in0=sb["embVpsb"][:, :],
                        in1=WRK[0:127, 0:64], op=ALU.subtract).then_inc(ve.h, 1)
        nc.vector.wait_ge(ve.h, ve.n + 1)
        v.tensor_tensor(out=sb["embVrow"][64:66, 0:64], in0=sb["embVrow"][64:66, 0:64],
                        in1=sb["embVrow"][64:66, 64:128], op=ALU.subtract).then_inc(ve.h, 1)
    ve.n += 2
    pro_ve = ve.n

    for s in range(BPC):
        sbase = s * T

        @block.gpsimd
        def _(g, tev=te.n):
            g.wait_ge(te.h, tev)
            g.wait_ge(ve.h, pro_ve)
            g.dma_start(out=wkv_w[:, :, :],
                        in_=dt["wkT"].ap().rearrange("(c p) n -> p c n", p=128)).then_inc(dma1.h, 16)
        dma1.n += 16
        wk_ld = dma1.n

        for tcq in range(4):
            t0 = sbase + tcq * 512

            @block.sync
            def _(sy, t0=t0, tev=te.n):
                sy.wait_ge(te.h, tev)
                sy.dma_start(out=xn512[:, :, :], in_=xn_ap[:, :, t0:t0 + 512]).then_inc(dma0.h, 16)
            dma0.n += 16
            xld = dma0.n
            for dc in range(NCH):
                @block.tensor
                def _(t, dc=dc, xld=xld, vev=ve.n):
                    t.wait_ge(dma0.h, xld)
                    t.wait_ge(dma1.h, wk_ld)
                    t.wait_ge(ve.h, vev)
                    inst = None
                    for c in range(NCH):
                        inst = t.matmul(E.pb(3), lhsT=wkv_w[:, c, dc * 128:(dc + 1) * 128],
                                        rhs=xn512[:, c, :], start=(c == 0), stop=False)
                    inst = t.matmul(E.pb(3), lhsT=sb["qkvob"][0:1, D + dc * 128:D + (dc + 1) * 128],
                                    rhs=onesb[0:1, 0:512], start=False, stop=True)
                    inst.then_inc(te.h, 1)
                te.n += 1

                @block.vector
                def _(v, dc=dc, tcq=tcq, tev=te.n):
                    v.wait_ge(te.h, tev)
                    v.tensor_copy(KB[:, dc, tcq * 512:(tcq + 1) * 512], E.pb(3)).then_inc(ve.h, 1)
                ve.n += 1

        @block.gpsimd
        def _(g, tev=te.n):
            g.wait_ge(te.h, tev)
            g.dma_start(out=wkv_w[:, :, :],
                        in_=dt["wvT"].ap().rearrange("(c p) n -> p c n", p=128)).then_inc(dma1.h, 16)
        dma1.n += 16
        wv_ld = dma1.n

        for tcq in range(4):
            t0 = sbase + tcq * 512

            @block.sync
            def _(sy, t0=t0, tev=te.n):
                sy.wait_ge(te.h, tev)
                sy.dma_start(out=xn512[:, :, :], in_=xn_ap[:, :, t0:t0 + 512]).then_inc(dma0.h, 16)
            dma0.n += 16
            xld = dma0.n
            for tt in range(4):
                kt = tcq * 4 + tt
                for dh in range(2):
                    @block.tensor
                    def _(t, tt=tt, dh=dh, xld=xld, vev=ve.n):
                        t.wait_ge(dma0.h, xld)
                        t.wait_ge(dma1.h, wv_ld)
                        t.wait_ge(ve.h, vev)
                        inst = None
                        for c in range(NCH):
                            inst = t.matmul(E.pb(2), lhsT=xn512[:, c, tt * 128:(tt + 1) * 128],
                                            rhs=wkv_w[:, c, dh * 512:(dh + 1) * 512],
                                            start=(c == 0), stop=False)
                        inst = t.matmul(E.pb(2), lhsT=onesb[0:1, 0:128],
                                        rhs=sb["qkvob"][0:1, 2 * D + dh * 512:2 * D + (dh + 1) * 512],
                                        start=False, stop=True)
                        inst.then_inc(te.h, 1)
                    te.n += 1

                    @block.vector
                    def _(v, kt=kt, dh=dh, tev=te.n):
                        v.wait_ge(te.h, tev)
                        dst = VA[:, kt, dh * 8:(dh + 1) * 8, 0:64]
                        v.tensor_copy(dst, E.pb(2).rearrange("p (h n) -> p h n", h=8)).then_inc(ve.h, 1)
                    ve.n += 1

        kv_ready = ve.n

        @block.gpsimd
        def _(g, tev=te.n):
            g.wait_ge(te.h, tev)
            g.dma_start(out=wq[:, :, :],
                        in_=dt["wqT"].ap().rearrange("(c p) n -> p c n", p=128)).then_inc(dma1.h, 16)
            g.dma_start(out=wo[:, :, :],
                        in_=dt["woT"].ap().rearrange("(c p) n -> p c n", p=128)).then_inc(dma1.h, 16)
        dma1.n += 32
        wqo_ld = dma1.n

        prev_wo = [te.n]
        prev_dnout = [dma2.n]
        for qt in range(16):
            q0 = qt * 128
            gq0 = sbase + q0
            near_lo = max(qt - 1, 0)
            near_hi = min(qt + 2, 16)
            near0 = near_lo * 128
            W = (near_hi - near_lo) * 128
            far_left = list(range(0, near_lo))
            far_right = list(range(near_hi, 16))
            neartiles = list(range(near_lo, near_hi))

            @block.sync
            def _(sy, gq0=gq0, tev=te.n):
                sy.wait_ge(te.h, tev)
                sy.dma_start(out=xnt[:, :, :], in_=xn_ap[:, :, gq0:gq0 + 128]).then_inc(dma0.h, 16)
            dma0.n += 16
            xtld = dma0.n

            for dc in range(NCH):
                @block.tensor
                def _(t, dc=dc, xtld=xtld, vev=ve.n, sev=se.n):
                    t.wait_ge(dma0.h, xtld)
                    t.wait_ge(dma1.h, wqo_ld)
                    t.wait_ge(ve.h, vev)
                    t.wait_ge(se.h, sev)
                    inst = None
                    for c in range(NCH):
                        inst = t.matmul(E.pb(5, 0, 128), lhsT=wq[:, c, dc * 128:(dc + 1) * 128],
                                        rhs=xnt[:, c, :], start=(c == 0), stop=False)
                    inst = t.matmul(E.pb(5, 0, 128), lhsT=sb["qkvob"][0:1, dc * 128:(dc + 1) * 128],
                                    rhs=onesb[0:1, 0:128], start=False, stop=True)
                    inst.then_inc(te.h, 1)
                te.n += 1

                @block.vector
                def _(v, dc=dc, tev=te.n):
                    v.wait_ge(te.h, tev)
                    v.tensor_copy(qtile[:, dc, :], E.pb(5, 0, 128)).then_inc(ve.h, 1)
                ve.n += 1
            q_ready = ve.n

            prev_bwT = [te.n]
            prev_wnT = [te.n]
            for h in range(NH):
                hb = (h % 2) * 64
                hc = h // 2
                Qh = qtile[hb:hb + 64, hc, :]
                odd = (h % 2) == 1

                @block.tensor
                def _(t, hb=hb, hc=hc, vev=ve.n, sev=se.n):
                    t.wait_ge(ve.h, max(q_ready, vev))
                    t.wait_ge(se.h, sev)
                    t.matmul(E.pb(1, 0, 129), lhsT=Qh, rhs=sb["embKTsb"][hb:hb + 64, :],
                             start=True, stop=True)
                    t.matmul(E.pb(1, 256, 384)[0:1, :], lhsT=sb["embKTsb"][hb:hb + 64, 0:1],
                             rhs=Qh, start=True, stop=True)
                    inst = t.matmul(E.pb(1, 384, 512)[0:1, :], lhsT=sb["embKTsb"][hb:hb + 64, 128:129],
                                    rhs=Qh, start=True, stop=True)
                    inst.then_inc(te.h, 1)
                te.n += 1
                qe_done = te.n

                @block.scalar
                def _(ss, qe_done=qe_done, vev=ve.n):
                    ss.wait_ge(te.h, qe_done)
                    ss.wait_ge(ve.h, vev)
                    ss.wait_ge(dma0.h, dma0.n)
                    ss.activation(out=qepad[:, 192:321], in_=E.pb(1, 0, 129),
                                  func=AF.Exp)
                    pbq = E.pb(1, 0, 129)
                    ends = bass.AP(tensor=pbq.tensor, offset=pbq.offset, ap=[[pbq.ap[0][0], 128], [128, 2]])
                    ss.activation(out=sb["qef32"][:, :], in_=ends, func=AF.Exp).then_inc(se.h, 1)
                se.n += 1
                qeexp_done = se.n

                @block.vector
                def _(v, qeexp_done=qeexp_done):
                    v.wait_ge(se.h, qeexp_done)
                    v.wait_ge(dma0.h, dma0.n)

                    def step(inst):
                        ve.inc(inst)
                    step(v.tensor_scalar(out=qepad[:, 0:192], in0=onesb[:, 0:192],
                                         scalar1=sb["qef32"][:, 0:1], scalar2=None, op0=ALU.mult))
                    step(v.tensor_scalar(out=qepad[:, 321:640], in0=onesb[:, 0:319],
                                         scalar1=sb["qef32"][:, 1:2], scalar2=None, op0=ALU.mult))
                    step(v.tensor_copy(qe0row, E.pb(1, 256, 512)[0:1, :]))
                    nc.vector.engine_nop().then_inc(ve.h, 1)
                ve.n += 1
                qepad_done = ve.n

                @block.sync
                def _(sy, qepad_done=qepad_done, near0=near0, q0=q0, W=W, prd=dma0.n):
                    sy.wait_ge(ve.h, qepad_done)
                    sy.wait_ge(dma0.h, prd)
                    sy.dma_start(out=dt["scr_qe"].ap()[:, :], in_=qepad).then_inc(dma0.h, 16)
                    sy.wait_ge(dma0.h, dma0.n + 16)
                    dap = bass.AP(tensor=dt["scr_qe"].ap().tensor,
                                  offset=PADL + near0 - q0, ap=[[639, 128], [1, W]])
                    sy.dma_start(out=erel[:, 0:W], in_=dap).then_inc(dma0.h, 16)
                dma0.n += 32
                erel_ld = dma0.n

                @block.tensor
                def _(t, hb=hb, hc=hc, near0=near0, W=W, sev=se.n, kvr=kv_ready):
                    t.wait_ge(se.h, sev)
                    t.wait_ge(ve.h, kvr)
                    inst = t.matmul(E.pb(2, 0, W), lhsT=Qh,
                                    rhs=KB[hb:hb + 64, hc, near0:near0 + W],
                                    start=True, stop=True)
                    inst.then_inc(te.h, 1)
                te.n += 1
                nearqk_done = te.n

                @block.scalar
                def _(ss, nearqk_done=nearqk_done, W=W, pwt=prev_wnT[0]):
                    ss.wait_ge(te.h, nearqk_done)
                    ss.wait_ge(te.h, pwt)
                    ss.wait_ge(dma0.h, dma0.n)
                    ss.activation(out=wnear[:, 0:W], in_=E.pb(2, 0, W), func=AF.Exp,
                                  bias=sb["shiftcol"][:, :], scale=1.0).then_inc(se.h, 1)
                se.n += 1
                wexp_done = se.n

                @block.vector
                def _(v, wexp_done=wexp_done, erel_ld=erel_ld, W=W):
                    v.wait_ge(se.h, wexp_done)
                    v.wait_ge(dma0.h, erel_ld)
                    v.tensor_tensor(out=wnear[:, 0:W], in0=wnear[:, 0:W], in1=erel[:, 0:W],
                                    op=ALU.mult).then_inc(ve.h, 1)
                ve.n += 1
                wnear_done_v = ve.n
                wnear_done = wnear_done_v

                @block.sync
                def _(sy, wnear_done=wnear_done, W=W, q0=q0, near0=near0, pbt=prev_bwT[0], prd=dma0.n, prol=dma1.n):
                    sy.wait_ge(ve.h, wnear_done)
                    sy.wait_ge(te.h, pbt)
                    sy.wait_ge(dma0.h, prd)
                    sy.wait_ge(dma1.h, prol)
                    sy.dma_start(out=dt["scr_wn"].ap()[:, 64:64 + W], in_=wnear[:, 0:W]).then_inc(dma0.h, 16)
                    sy.wait_ge(dma0.h, dma0.n + 16)
                    dap = bass.AP(tensor=dt["scr_wn"].ap().tensor,
                                  offset=q0 - near0 + 1, ap=[[641, 128], [1, 127]])
                    sy.dma_start(out=bwt[:, 0:127], in_=dap).then_inc(dma0.h, 16)
                dma0.n += 32
                bw_ld = dma0.n

                @block.tensor
                def _(t, wnear_done=wnear_done, vev=ve.n, W=W):
                    t.wait_ge(ve.h, wnear_done)
                    t.wait_ge(ve.h, vev)
                    inst = None
                    for j in range(W // 128):
                        inst = t.transpose(PBbf[4][:, j * 128:(j + 1) * 128],
                                           in_=wnear[:, j * 128:(j + 1) * 128],
                                           identity=sb["identsb"][:, :])
                    inst.then_inc(te.h, 1)
                te.n += 1
                wnT_done = te.n
                prev_wnT[0] = wnT_done

                @block.vector
                def _(v, wnT_done=wnT_done, W=W):
                    v.wait_ge(te.h, wnT_done)
                    v.tensor_copy(wnearT[:, 0:W], PBbf[4][:, 0:W]).then_inc(ve.h, 1)
                ve.n += 1
                wnT_evac = ve.n

                fars = far_left + far_right
                batches = [fars[i:i + 4] for i in range(0, len(fars), 4)]
                @block.tensor
                def _(t, odd=odd, vev=ve.n):
                    t.wait_ge(ve.h, vev)
                    if not odd:
                        inst = t.matmul(E.pb(0, 0, 128)[0:66, :], lhsT=sb["zrow"][0:1, 0:66],
                                        rhs=onesb[0:1, 0:128], start=True, stop=False)
                    else:
                        inst = t.matmul(E.pb(0, 0, 128)[64:128, :], lhsT=sb["zrow"][0:1, 0:64],
                                        rhs=onesb[0:1, 0:128], start=True, stop=False)
                        inst = t.matmul(E.pb(5, 128, 256)[64:66, :], lhsT=sb["zrow"][0:1, 0:2],
                                        rhs=onesb[0:1, 0:128], start=True, stop=False)
                    inst.then_inc(te.h, 1)
                te.n += 1

                def emit_pv(ktile, rhs_tile, rhs_off, src_ready_sem, src_ready_val, is_right):
                    @block.tensor
                    def _(t, ktile=ktile, rhs_tile=rhs_tile, rhs_off=rhs_off,
                          src_ready_sem=src_ready_sem, src_ready_val=src_ready_val,
                          is_right=is_right, h=h, odd=odd):
                        t.wait_ge(src_ready_sem.h, src_ready_val)
                        rr = rhs_tile[:, rhs_off:rhs_off + 128]
                        if not odd:
                            ncols = 66 if is_right else 65
                            inst = t.matmul(E.pb(0, 0, 128)[0:ncols, :],
                                            lhsT=VA[:, ktile, h, 0:ncols], rhs=rr,
                                            start=False, stop=False)
                        else:
                            inst = t.matmul(E.pb(0, 0, 128)[64:128, :],
                                            lhsT=VA[:, ktile, h, 0:64], rhs=rr,
                                            start=False, stop=False)
                            ncols = 2 if is_right else 1
                            inst = t.matmul(E.pb(5, 128, 256)[64:64 + ncols, :],
                                            lhsT=VA[:, ktile, h, 64:64 + ncols], rhs=rr,
                                            start=False, stop=False)
                        inst.then_inc(te.h, 1)
                    te.n += 1

                for bi, batch in enumerate(batches):
                    ebuf = bi % 2

                    @block.tensor
                    def _(t, batch=batch, sev=se.n, h=h, hb=hb, hc=hc, qpd=qepad_done):
                        t.wait_ge(se.h, sev)
                        t.wait_ge(ve.h, qpd)
                        inst = None
                        for j, ktile in enumerate(batch):
                            inst = t.matmul(E.pb(3, j * 128, (j + 1) * 128),
                                            lhsT=KB[hb:hb + 64, hc, ktile * 128:(ktile + 1) * 128],
                                            rhs=Qh, start=True, stop=False)
                            roff = 0 if ktile < qt else 128
                            inst = t.matmul(E.pb(3, j * 128, (j + 1) * 128),
                                            lhsT=onesb[0:1, 0:128],
                                            rhs=qe0row[0:1, roff:roff + 128],
                                            start=False, stop=True)
                        inst.then_inc(te.h, 1)
                    te.n += 1
                    fqk_done = te.n

                    @block.scalar
                    def _(ss, batch=batch, ebuf=ebuf, fqk_done=fqk_done, qepad_done=qepad_done, tev=te.n):
                        ss.wait_ge(te.h, fqk_done)
                        ss.wait_ge(ve.h, qepad_done)
                        ss.wait_ge(te.h, tev)
                        ss.activation(out=eqkT[:, ebuf * 512:ebuf * 512 + len(batch) * 128],
                                      in_=E.pb(3, 0, len(batch) * 128), func=AF.Exp,
                                      bias=sb["shiftcol"][:, :], scale=1.0).then_inc(se.h, 1)
                    se.n += 1
                    fexp_done = se.n

                    if "fdbg" in E.dbg and s == 0 and qt == 5 and h == 0 and bi == 0:
                        @block.sync
                        def _(sy, fexp_done=fexp_done):
                            sy.wait_ge(se.h, fexp_done)
                            sy.dma_start(out=dt["dbg_eqk"].ap(), in_=eqkT).then_inc(dma2.h, 16)
                            sy.dma_start(out=dt["dbg_qe0"].ap(), in_=qe0row).then_inc(dma2.h, 16)
                        dma2.n += 32

                    for j, ktile in enumerate(batch):
                        emit_pv(ktile, eqkT, ebuf * 512 + j * 128, se, fexp_done,
                                is_right=(ktile >= near_hi))

                for j, ktile in enumerate(neartiles):
                    emit_pv(ktile, wnearT, j * 128, ve, wnT_evac, is_right=False)

                if "pdbg" in E.dbg and s == 0 and qt == 5 and h == 2:
                    @block.vector
                    def _(v, tev=te.n):
                        v.wait_ge(te.h, tev)
                        v.tensor_copy(WRK[0:66, 5632:5760], E.pb(0, 0, 128)[0:66, :]).then_inc(ve.h, 1)
                    ve.n += 1
                    dbg_cp1 = ve.n

                    @block.sync
                    def _(sy, vev=ve.n):
                        sy.wait_ge(ve.h, vev)
                        sy.dma_start(out=dt["dbg_pv"].ap(), in_=WRK[0:66, 5632:5760]).then_inc(dma2.h, 16)
                    dma2.n += 16

                ragspecs = []
                for j, ktile in enumerate(neartiles):
                    th = q0 + 64 - (near0 + 128 * j)
                    if th >= 128:
                        continue
                    ragspecs.append((j, 0 if th == 64 else (1 if th == -64 else None)))
                if SKIP_RAGGED:
                    ragspecs = []
                for ri, (j, mcol) in enumerate(ragspecs):
                    last = ri == len(ragspecs) - 1
                    if mcol is not None:
                        @block.vector
                        def _(v, j=j, mcol=mcol, wnT_evac=wnT_evac, tev=te.n):
                            v.wait_ge(ve.h, wnT_evac)
                            v.wait_ge(te.h, tev)
                            v.tensor_tensor(out=mskd, in0=wnearT[:, j * 128:(j + 1) * 128],
                                            in1=sb["rmasksb"][:, mcol * 128:(mcol + 1) * 128],
                                            op=ALU.mult).then_inc(ve.h, 1)
                        ve.n += 1
                        wv = ve.n
                    else:
                        wv = wnT_evac

                    @block.tensor
                    def _(t, j=j, mcol=mcol, wv=wv, odd=odd, last=last):
                        t.wait_ge(ve.h, wv)
                        rr = mskd if mcol is not None else wnearT[:, j * 128:(j + 1) * 128]
                        dst = E.pb(0, 0, 128)[64:66, :] if not odd else E.pb(5, 128, 256)[64:66, :]
                        inst = t.matmul(dst, lhsT=sb["z1col"][:, :], rhs=rr,
                                        start=False, stop=(last and odd))
                        inst.then_inc(te.h, 1)
                    te.n += 1

                if qt == 0 or qt == 15:
                    @block.vector
                    def _(v, qt=qt, bw_ld=bw_ld):
                        v.wait_ge(dma0.h, bw_ld)
                        mc = 0 if qt == 0 else 127
                        v.tensor_tensor(out=bwt[:, 0:127], in0=bwt[:, 0:127],
                                        in1=sb["bwmasksb"][:, mc:mc + 127], op=ALU.mult).then_inc(ve.h, 1)
                    ve.n += 1
                    bw_ready_v = ve.n
                    bw_sem, bw_val = ve, bw_ready_v
                else:
                    bw_sem, bw_val = dma0, bw_ld

                @block.tensor
                def _(t, bw_sem=bw_sem, bw_val=bw_val, vev=ve.n):
                    t.wait_ge(bw_sem.h, bw_val)
                    t.wait_ge(ve.h, vev)
                    inst = t.transpose(PBbf[4][:, 384:512][0:127, :], in_=bwt[:, 0:127],
                                       identity=sb["identsb"][:, :])
                    inst.then_inc(te.h, 1)
                te.n += 1
                bwT_done = te.n
                prev_bwT[0] = bwT_done

                @block.vector
                def _(v, bwT_done=bwT_done):
                    v.wait_ge(te.h, bwT_done)
                    v.tensor_copy(bwTs[0:127, :], PBbf[4][:, 384:512][0:127, :]).then_inc(ve.h, 1)
                ve.n += 1
                bwT_evac = ve.n

                @block.tensor
                def _(t, bwT_evac=bwT_evac):
                    t.wait_ge(ve.h, bwT_evac)
                    inst = t.matmul(E.pb(0, 0, 128)[0:64, :] if not odd else E.pb(0, 0, 128)[64:128, :],
                                    lhsT=sb["embVpsb"][:, :], rhs=bwTs[0:127, :],
                                    start=False, stop=False)
                    inst.then_inc(te.h, 1)
                te.n += 1
                bwmm_done = te.n

                if "pdbg" in E.dbg and s == 0 and qt == 5 and h == 2:
                    @block.vector
                    def _(v, bwmm_done=bwmm_done):
                        v.wait_ge(te.h, bwmm_done)
                        v.tensor_copy(WRK[0:66, 5760:5888], E.pb(0, 0, 128)[0:66, :]).then_inc(ve.h, 1)
                    ve.n += 1

                    @block.sync
                    def _(sy, vev=ve.n):
                        sy.wait_ge(ve.h, vev)
                        sy.dma_start(out=dt["dbg_pv2"].ap(), in_=WRK[0:66, 5760:5888]).then_inc(dma2.h, 16)
                    dma2.n += 16

                @block.vector
                def _(v, odd=odd, h=h, bwmm_done=bwmm_done, pdn=prev_dnout[0]):
                    v.wait_ge(te.h, bwmm_done)
                    v.wait_ge(dma2.h, pdn)
                    rows_src = E.pb(0, 0, 128)[64:66, :] if not odd else E.pb(5, 128, 256)[64:66, :]
                    v.tensor_copy(rowbuf, rows_src).then_inc(ve.h, 1)
                    nc.vector.wait_ge(ve.h, ve.n + 1)
                    v.tensor_copy(dnrow[:, h * 128:(h + 1) * 128], rowbuf[0:1, :]).then_inc(ve.h, 1)
                ve.n += 2
                rows_done = ve.n

                if "fdbg" in E.dbg and s == 0 and qt == 5 and h == 0:
                    @block.sync
                    def _(sy, rd=rows_done):
                        sy.wait_ge(ve.h, rd)
                        sy.dma_start(out=dt["dbg_rows"].ap()[64:66, :], in_=rowbuf).then_inc(dma2.h, 16)
                        sy.dma_start(out=dt["dbg_wnt"].ap(), in_=wnearT[:, 0:384]).then_inc(dma2.h, 16)
                    dma2.n += 32

                @block.tensor
                def _(t, rows_done=rows_done, odd=odd):
                    t.wait_ge(ve.h, rows_done)
                    dst = E.pb(0, 0, 128)[0:64, :] if not odd else E.pb(0, 0, 128)[64:128, :]
                    inst = t.matmul(dst, lhsT=sb["embVrow"][64:66, 0:64], rhs=rowbuf,
                                    start=False, stop=True)
                    inst.then_inc(te.h, 1)
                te.n += 1
                tails_done = te.n

                @block.vector
                def _(v, tails_done=tails_done, odd=odd, hc=hc, pwo=prev_wo[0]):
                    v.wait_ge(te.h, tails_done)
                    v.wait_ge(te.h, pwo)
                    src = E.pb(0, 0, 128)[0:64, :] if not odd else E.pb(0, 0, 128)[64:128, :]
                    dst = attnraw[0:64, hc, :] if not odd else attnraw[64:128, hc, :]
                    v.tensor_copy(dst, src).then_inc(ve.h, 1)
                ve.n += 1
            heads_done_ve = ve.n

            @block.vector
            def _(v, heads_done_ve=heads_done_ve):
                v.wait_ge(ve.h, heads_done_ve)
                v.reciprocal(out=dnrow, in_=dnrow).then_inc(ve.h, 1)
            ve.n += 1
            recip_done = ve.n

            @block.gpsimd
            def _(g, recip_done=recip_done, vev=ve.n):
                g.wait_ge(ve.h, recip_done)
                g.dma_start(out=dt["scr_dn"].ap()[:, :], in_=dnrow).then_inc(dma2.h, 16)
            dma2.n += 16
            dn_out = dma2.n
            prev_dnout[0] = dn_out

            @block.gpsimd
            def _(g, dn_out=dn_out, vev=ve.n):
                g.wait_ge(dma2.h, dn_out)
                g.wait_ge(ve.h, vev)
                sd = dt["scr_dn"].ap()
                for a in range(2):
                    ap = bass.AP(tensor=sd.tensor, offset=a * 128,
                                 ap=[[0, 64], [256, 8], [1, 128]])
                    g.dma_start(out=sb["recipbc"][a * 64:(a + 1) * 64, :, :], in_=ap).then_inc(dma2.h, 16)
            dma2.n += 32
            bc_ld = dma2.n

            @block.vector
            def _(v, bc_ld=bc_ld, heads_done_ve=heads_done_ve):
                v.wait_ge(dma2.h, bc_ld)
                v.wait_ge(ve.h, heads_done_ve)
                v.tensor_tensor(out=attnraw[:, :, :], in0=attnraw[:, :, :],
                                in1=sb["recipbc"][:, :, :], op=ALU.mult).then_inc(ve.h, 1)
            ve.n += 1
            apply_done = ve.n

            if "rdbg" in E.dbg and s == 0 and qt == 0:
                @block.sync
                def _(sy, ad=apply_done):
                    sy.wait_ge(ve.h, ad)
                    sy.dma_start(out=dt["dbg_recipbc"].ap(), in_=sb["recipbc"][:, :, :]).then_inc(dma2.h, 16)
                    sy.dma_start(out=dt["dbg_dn"].ap()[0:1, :], in_=dnrow).then_inc(dma2.h, 16)
                    sy.dma_start(out=dt["dbg_araw"].ap(), in_=attnraw[:, :, :]).then_inc(dma2.h, 16)
                dma2.n += 48

            for dc in range(NCH):
                bank = 6 + dc // 4
                lo = (dc % 4) * 128

                @block.tensor
                def _(t, dc=dc, bank=bank, lo=lo, apply_done=apply_done, vev=ve.n):
                    t.wait_ge(ve.h, apply_done)
                    t.wait_ge(ve.h, vev)
                    inst = None
                    for c in range(NCH):
                        inst = t.matmul(E.pb(bank, lo, lo + 128), lhsT=wo[:, c, dc * 128:(dc + 1) * 128],
                                        rhs=attnraw[:, c, :], start=(c == 0), stop=False)
                    inst = t.matmul(E.pb(bank, lo, lo + 128),
                                    lhsT=sb["qkvob"][0:1, 3 * D + dc * 128:3 * D + (dc + 1) * 128],
                                    rhs=onesb[0:1, 0:128], start=False, stop=True)
                    inst.then_inc(te.h, 1)
                te.n += 1

                @block.vector
                def _(v, dc=dc, bank=bank, lo=lo, gq0=gq0, tev=te.n):
                    v.wait_ge(te.h, tev)
                    dst = ACC2[:, dc, gq0:gq0 + 128]
                    v.tensor_tensor(out=dst, in0=dst, in1=E.pb(bank, lo, lo + 128),
                                    op=ALU.add).then_inc(ve.h, 1)
                ve.n += 1
            prev_wo[0] = te.n

            if "awrk" in E.dbg and s == 0 and qt == 5:
                @block.sync
                def _(sy, vev=ve.n):
                    sy.wait_ge(ve.h, vev)
                    sy.dma_start(out=dt["dbg_wrk"].ap(), in_=WRK[:, :]).then_inc(dma2.h, 16)
                    sy.dma_start(out=dt["dbg_wrkf"].ap(), in_=sb["WRKF"][:, :]).then_inc(dma2.h, 16)
                    sy.dma_start(out=dt["dbg_va"].ap(), in_=sb["VBUF"][:, 0:16 * 16 * 66]).then_inc(dma2.h, 16)
                    sy.dma_start(out=dt["dbg_kb"].ap(), in_=KB[:, :, :]).then_inc(dma2.h, 16)
                    sy.dma_start(out=dt["dbg_evp"].ap(), in_=sb["embVpsb"][:, :]).then_inc(dma2.h, 16)
                    sy.dma_start(out=dt["dbg_evrow"].ap()[64:66, :], in_=sb["embVrow"][64:66, :]).then_inc(dma2.h, 16)
                dma2.n += 96

    if "l0sum" in E.dbg:
        pass
    if "l0sum" in E.dbg:
        dbg_ap = dt["dbg_l0sum"].ap().rearrange("(c p) n -> p c n", p=128)

        @block.sync
        def _(sy):
            sy.wait_ge(ve.h, ve.n)
            sy.dma_start(out=dbg_ap[:, :, :], in_=ACC2[:, :, :]).then_inc(dma2.h, 16)
        dma2.n += 16




def emit_resln(E):
    nc, block, dt, sb, sems = E.nc, E.block, E.dt, E.sb, E.sems
    dma0, dma1, dma2 = sems["dma0"], sems["dma1"], sems["dma2"]
    te, ve, se, ge = sems["te"], sems["ve"], sems["se"], sems["ge"]
    x_ap = dt["x_fm"].ap().rearrange("(c p) n -> p c n", p=128)
    x1_ap = dt["x1_dram"].ap().rearrange("(c p) n -> p c n", p=128)
    xn_ap = dt["xn_dram"].ap().rearrange("(c p) n -> p c n", p=128)
    onesb, ident = sb["onesb"], sb["identsb"]
    xb = sb["WRK"].rearrange("p (c n) -> p c n", c=16)
    strow = sb["WRKF"][:, 0:16]
    stT8 = sb["WRKF"][0:8, 256:384]
    bc_m = sb["WRKF"][:, 512:576].bitcast(BF16)
    bc_r = sb["WRKF"][:, 576:640].bitcast(BF16)
    ACC2 = sb["ACC2"]

    NCHK = NTOK // TCH
    for ck in range(NCHK):
        t0 = ck * TCH

        @block.gpsimd
        def _(g, ck=ck, t0=t0, tev=te.n, sev=se.n, vev=ve.n, dv=dma0.n):
            g.wait_ge(te.h, tev)
            g.wait_ge(se.h, sev)
            g.wait_ge(ve.h, vev)
            g.wait_ge(dma0.h, dv)
            g.dma_start(out=xb[:, 0:8, :], in_=x_ap[:, :, t0:t0 + TCH]).then_inc(dma1.h, 16)
        dma1.n += 16
        ld = dma1.n

        @block.vector
        def _(v, ld=ld, t0=t0):
            v.wait_ge(dma1.h, ld)
            inst = None
            for c in range(NCH):
                inst = v.tensor_tensor(out=xb[:, c, :], in0=xb[:, c, :],
                                       in1=ACC2[:, c, t0:t0 + TCH], op=ALU.add)
            inst.then_inc(ve.h, 1)
        ve.n += 1
        x1_done = ve.n

        @block.sync
        def _(sy, x1_done=x1_done, t0=t0):
            sy.wait_ge(ve.h, x1_done)
            sy.dma_start(out=x1_ap[:, :, t0:t0 + TCH], in_=xb[:, 0:8, :]).then_inc(dma0.h, 16)
        dma0.n += 16

        @block.scalar
        def _(s, x1_done=x1_done):
            s.wait_ge(ve.h, x1_done)
            inst = None
            for c in range(NCH):
                inst = s.activation(out=xb[:, 8 + c, :], in_=xb[:, c, :], func=AF.Square)
            inst.then_inc(se.h, 1)
        se.n += 1
        sq_done = se.n

        @block.tensor
        def _(t, x1_done=x1_done, sq_done=sq_done, vev=ve.n):
            t.wait_ge(ve.h, x1_done)
            t.wait_ge(ge.h, E.ones_ready)
            inst = None
            for tt in range(4):
                for c in range(NCH):
                    inst = t.matmul(E.pb(0, tt, tt + 1),
                                    lhsT=xb[:, c, tt * 128:(tt + 1) * 128],
                                    rhs=onesb[:, 0:1], start=(c == 0), stop=(c == NCH - 1))
            t.wait_ge(se.h, sq_done)
            for tt in range(4):
                for c in range(NCH):
                    inst = t.matmul(E.pb(0, 8 + tt, 9 + tt),
                                    lhsT=xb[:, 8 + c, tt * 128:(tt + 1) * 128],
                                    rhs=onesb[:, 0:1], start=(c == 0), stop=(c == NCH - 1))
            inst.then_inc(te.h, 1)
        te.n += 1
        mm_stats = te.n

        @block.vector
        def _(v, mm_stats=mm_stats):
            v.wait_ge(te.h, mm_stats)

            def step(inst):
                ve.inc(inst)
                v.wait_ge(ve.h, ve.n)
            step(v.tensor_scalar(out=strow[:, 0:4], in0=E.pb(0, 0, 4), scalar1=1.0 / D,
                                 scalar2=None, op0=ALU.mult))
            step(v.tensor_scalar(out=strow[:, 8:12], in0=E.pb(0, 8, 12), scalar1=1.0 / D,
                                 scalar2=None, op0=ALU.mult))
            step(v.tensor_tensor(out=strow[:, 12:16], in0=strow[:, 0:4], in1=strow[:, 0:4],
                                 op=ALU.mult))
            step(v.scalar_tensor_tensor(out=strow[:, 8:12], in0=strow[:, 12:16], scalar=-1.0,
                                        in1=strow[:, 8:12], op0=ALU.mult, op1=ALU.add))
            step(v.tensor_scalar(out=strow[:, 8:12], in0=strow[:, 8:12], scalar1=1e-5,
                                 scalar2=None, op0=ALU.add))
            quake_rsqrt(v, ve, strow[:, 4:8], strow[:, 8:12], strow[:, 12:16])
        stats_done = ve.n

        @block.tensor
        def _(t, stats_done=stats_done):
            t.wait_ge(ve.h, stats_done)
            t.transpose(E.pb(1, 0, 128)[0:8, :], in_=strow[:, 0:8],
                        identity=sb["identf"][:, :]).then_inc(te.h, 1)
        te.n += 1

        @block.vector
        def _(v, tev=te.n):
            v.wait_ge(te.h, tev)
            v.tensor_copy(stT8[:, :], E.pb(1, 0, 128)[0:8, :]).then_inc(ve.h, 1)
        ve.n += 1

        @block.sync
        def _(sy, vev=ve.n, pbc=dma2.n):
            sy.wait_ge(ve.h, vev)
            sy.wait_ge(dma2.h, pbc)
            sy.dma_start(out=dt["stat_dram"].ap()[:, :], in_=stT8[:, :]).then_inc(dma2.h, 16)
        dma2.n += 16
        stat_w = dma2.n

        for tt in range(4):
            @block.gpsimd
            def _(g, tt=tt, stat_w=stat_w, vev=ve.n):
                g.wait_ge(dma2.h, stat_w)
                g.wait_ge(ve.h, vev)
                sd = dt["stat_dram"].ap()
                mrow = bass.AP(tensor=sd.tensor, offset=tt * 128, ap=[[0, 128], [1, 128]])
                rrow = bass.AP(tensor=sd.tensor, offset=(4 + tt) * 128, ap=[[0, 128], [1, 128]])
                g.dma_start(out=bc_m, in_=mrow).then_inc(dma2.h, 16)
                g.dma_start(out=bc_r, in_=rrow).then_inc(dma2.h, 16)
            dma2.n += 32
            bc_ld = dma2.n

            @block.vector
            def _(v, tt=tt, bc_ld=bc_ld):
                v.wait_ge(dma2.h, bc_ld)

                def step(inst):
                    ve.inc(inst)
                v.wait_ge(ve.h, ve.n)
                for c in range(NCH):
                    sl = xb[:, c, tt * 128:(tt + 1) * 128]
                    step(v.tensor_tensor(out=sl, in0=sl, in1=bc_m, op=ALU.subtract))
                v.wait_ge(ve.h, ve.n)
                for c in range(NCH):
                    sl = xb[:, c, tt * 128:(tt + 1) * 128]
                    step(v.tensor_tensor(out=sl, in0=sl, in1=bc_r, op=ALU.mult))
                v.wait_ge(ve.h, ve.n)
                nc.vector.engine_nop().then_inc(ve.h, 1)
            ve.n += 1

        xn_done = ve.n

        @block.sync
        def _(sy, xn_done=xn_done, t0=t0):
            sy.wait_ge(ve.h, xn_done)
            sy.dma_start(out=xn_ap[:, :, t0:t0 + TCH], in_=xb[:, 0:8, :]).then_inc(dma0.h, 16)
        dma0.n += 16

    E.xn_written = dma0.n
    if "x1" in E.dbg:
        @block.sync
        def _(sy, dv=dma0.n):
            sy.wait_ge(dma0.h, dv)
            xd = dt["dbg_x1"].ap()
            sy.dma_start(out=xd[:, :], in_=dt["x1_dram"].ap()[:, :]).then_inc(dma2.h, 16)
            xd2 = dt["dbg_xn2"].ap()
            sy.dma_start(out=xd2[:, :], in_=dt["xn_dram"].ap()[:, :]).then_inc(dma2.h, 16)
        dma2.n += 32


def emit_final(E):
    nc, block, dt, sb, sems = E.nc, E.block, E.dt, E.sb, E.sems
    dma0, dma1 = sems["dma0"], sems["dma1"]
    te, ve, se = sems["te"], sems["ve"], sems["se"]
    x1_ap = dt["x1_dram"].ap().rearrange("(c p) n -> p c n", p=128)
    out_ap = dt["out_fm"].ap().rearrange("(c p) n -> p c n", p=128)
    ACC2 = sb["ACC2"]
    xb = sb["WRK"].rearrange("p (c n) -> p c n", c=16)
    ob = sb["WRKF"][:, 0:1024].rearrange("p (c n) -> p c n", c=8)

    for ck in range(NTOK // 128):
        t0 = ck * 128

        @block.sync
        def _(sy, t0=t0, tev=te.n, vev=ve.n, ck=ck):
            sy.wait_ge(ve.h, vev)
            sy.wait_ge(te.h, tev)
            sy.dma_start(out=xb[:, 0:8, 0:128], in_=x1_ap[:, :, t0:t0 + 128]).then_inc(dma0.h, 16)
        dma0.n += 16
        ld = dma0.n

        @block.vector
        def _(v, ld=ld, t0=t0):
            v.wait_ge(dma0.h, ld)
            inst = None
            for c in range(NCH):
                inst = v.tensor_tensor(out=ob[:, c, :], in0=xb[:, c, 0:128],
                                       in1=ACC2[:, c, t0:t0 + 128], op=ALU.add)
            inst.then_inc(ve.h, 1)
        ve.n += 1

        @block.sync
        def _(sy, vev=ve.n, t0=t0):
            sy.wait_ge(ve.h, vev)
            sy.dma_start(out=out_ap[:, :, t0:t0 + 128], in_=ob[:, :, :]).then_inc(dma0.h, 16)
        dma0.n += 16




def emit_l1ffn(E):
    nc, block, dt, sb, sems = E.nc, E.block, E.dt, E.sb, E.sems
    dma0, dma1, dma2 = sems["dma0"], sems["dma1"], sems["dma2"]
    te, ve, se, ge = sems["te"], sems["ve"], sems["se"], sems["ge"]
    onesb, identf = sb["onesb"], sb["identf"]
    xn_ap = dt["xn_dram"].ap().rearrange("(c p) n -> p c n", p=128)
    ACC2 = sb["ACC2"]
    xnbuf = sb["WRKF"].bitcast(BF16)[:, 0:4096].rearrange("p (c n) -> p c n", c=8)
    h_sb = sb["WRK"].rearrange("p (c n) -> p c n", c=16)
    cwbc = sb["cwbc1"][:, :]
    WF32 = sb["WRKF"]
    probs = WF32[:, 1024:1028]
    scr1 = WF32[:, 1028:1032]
    scr2 = WF32[:, 1032:1036]
    cwf = WF32[:, 1040:1044]
    cols = WF32[:, 1044:1052]
    cwrow = WF32[0:4, 1056:1184]
    w1slots = [sb["KBUF"].rearrange("p c n -> p (c n)"), sb["VBUF"][:, 0:16384]]
    w2slot = sb["XNS"].rearrange("p c n -> p (c n)")

    @block.gpsimd
    def _(g, tev=te.n, vev=ve.n):
        g.wait_ge(te.h, tev)
        g.wait_ge(ve.h, vev)
        g.dma_start(out=sb["fb1sb"][:, :, :],
                    in_=dt["fb1T"].ap().rearrange("(c p) e -> p c e", p=128)).then_inc(dma1.h, 16)
        g.dma_start(out=sb["eb2msb"][0:6, :], in_=dt["fb2m"].ap()).then_inc(dma1.h, 16)
        g.dma_start(out=sb["r0Wsb"][:, :, 0:4],
                    in_=dt["g1_w"].ap().rearrange("(c p) e -> p c e", p=128)).then_inc(dma1.h, 16)
        g.dma_start(out=sb["rowsb"][0:1, 0:4], in_=dt["g1_b"].ap().unsqueeze(0)).then_inc(dma1.h, 16)
        g.memset(sb["cwxsb"][:, :], 0.5)
    dma1.n += 64
    pro_ld = dma1.n

    for tt in range(NTT):
        t0 = tt * 128

        @block.sync
        def _(sy, t0=t0, tev=te.n, dv=E.xn_written):
            sy.wait_ge(dma0.h, dv)
            sy.wait_ge(te.h, tev)
            sy.dma_start(out=xnbuf[:, :, 0:128], in_=xn_ap[:, :, t0:t0 + 128]).then_inc(dma0.h, 16)
        dma0.n += 16
        xld = dma0.n

        @block.tensor
        def _(t, xld=xld, tt=tt, vev=ve.n, sev=se.n):
            t.wait_ge(dma0.h, xld)
            t.wait_ge(dma1.h, pro_ld)
            if tt > 0:
                t.wait_ge(ve.h, vev)
                t.wait_ge(se.h, sev)
            inst = None
            for c in range(NCH):
                inst = t.matmul(E.pb(6, 0, 4), lhsT=xnbuf[:, c, 0:128],
                                rhs=sb["r0Wsb"][:, c, 0:4], start=(c == 0), stop=False)
            inst = t.matmul(E.pb(6, 0, 4), lhsT=onesb[0:1, 0:128],
                            rhs=sb["rowsb"][0:1, 0:4], start=False, stop=True)
            inst.then_inc(te.h, 1)
        te.n += 1
        lg_done = te.n

        @block.vector
        def _(v, lg_done=lg_done):
            v.wait_ge(te.h, lg_done)

            def step(inst):
                ve.inc(inst)
                v.wait_ge(ve.h, ve.n)
            step(v.tensor_scalar(out=scr1, in0=E.pb(6, 0, 4), scalar1=0.0, scalar2=-3e38,
                                 op0=ALU.add, op1=ALU.max, accum_out=cols[:, 0:1]))
            step(v.tensor_scalar(out=cols[:, 1:2], in0=cols[:, 0:1], scalar1=-1.0,
                                 scalar2=None, op0=ALU.mult))
        @block.scalar
        def _(s, vev=ve.n):
            s.wait_ge(ve.h, vev)
            s.activation(out=probs, in_=E.pb(6, 0, 4), func=AF.Exp, bias=cols[:, 1:2],
                         scale=1.0, accum_out=cols[:, 2:3]).then_inc(se.h, 1)
        se.n += 1

        @block.vector
        def _(v, sev=se.n):
            v.wait_ge(se.h, sev)

            def step(inst):
                ve.inc(inst)
                v.wait_ge(ve.h, ve.n)
            step(v.reciprocal(out=cols[:, 3:4], in_=cols[:, 2:3]))
            step(v.tensor_scalar(out=probs, in0=probs, scalar1=cols[:, 3:4], scalar2=None,
                                 op0=ALU.mult))
            step(v.tensor_scalar(out=scr1, in0=probs, scalar1=0.0, scalar2=-3e38,
                                 op0=ALU.add, op1=ALU.max, accum_out=cols[:, 4:5]))
            step(v.scalar_tensor_tensor(out=scr2, in0=probs, scalar=cols[:, 4:5], in1=probs,
                                        op0=ALU.is_lt, op1=ALU.mult))
            step(v.tensor_scalar(out=scr1, in0=scr2, scalar1=0.0, scalar2=-3e38,
                                 op0=ALU.add, op1=ALU.max, accum_out=cols[:, 5:6]))
            step(v.scalar_tensor_tensor(out=cwf, in0=probs, scalar=cols[:, 5:6], in1=probs,
                                        op0=ALU.is_ge, op1=ALU.mult))
        cw_ready = ve.n

        @block.tensor
        def _(t, cw_ready=cw_ready):
            t.wait_ge(ve.h, cw_ready)
            t.transpose(E.pb(7, 0, 128)[0:4, :], in_=cwf, identity=identf[:, :]).then_inc(te.h, 1)
        te.n += 1

        @block.vector
        def _(v, tev=te.n):
            v.wait_ge(te.h, tev)
            v.tensor_copy(cwrow, E.pb(7, 0, 128)[0:4, :]).then_inc(ve.h, 1)
        ve.n += 1

        @block.sync
        def _(sy, tt=tt, t0=t0, vev=ve.n):
            sy.wait_ge(ve.h, vev)
            sy.dma_start(out=dt["cw_dram"].ap()[0:4, t0:t0 + 128], in_=cwrow).then_inc(dma2.h, 16)
        dma2.n += 16
    gate_done_dma = dma2.n

    TC2 = 512
    NC2 = NTOK // TC2
    subs = []
    for n_ in range(NS1):
        for half in range(2):
            subs.append(("s", n_, half))
    for r_ in range(NE1):
        for half in range(2):
            subs.append(("r", r_, half))

    def wsrc(sub):
        kind, widx, half = sub
        if kind == "s":
            w1 = dt["fsw1"].ap()[widx][:, half * 2048:(half + 1) * 2048]
            w2 = dt["fsw2"].ap()[widx][half * 2048:(half + 1) * 2048, :]
        else:
            w1 = dt["frw1"].ap()[widx][:, half * 2048:(half + 1) * 2048]
            w2 = dt["frw2"].ap()[widx][half * 2048:(half + 1) * 2048, :]
        return w1, w2

    w1load = {}
    w2load = {}

    def emit_w1load(si, after_te):
        @block.gpsimd
        def _(g, si=si, after_te=after_te):
            g.wait_ge(te.h, after_te)
            w1, _ = wsrc(subs[si])
            g.dma_start(out=w1slots[si % 2].rearrange("p (c n) -> p c n", c=8),
                        in_=w1.rearrange("(c p) n -> p c n", p=128)).then_inc(dma1.h, 16)
        dma1.n += 16
        w1load[si] = dma1.n

    def emit_w2load(si, after_te):
        @block.gpsimd
        def _(g, si=si, after_te=after_te):
            g.wait_ge(te.h, after_te)
            _, w2 = wsrc(subs[si])
            g.dma_start(out=w2slot.rearrange("p (c n) -> p c n", c=16),
                        in_=w2.rearrange("(c p) n -> p c n", p=128)).then_inc(dma1.h, 16)
        dma1.n += 16
        w2load[si] = dma1.n

    emit_w1load(0, te.n)
    emit_w2load(0, te.n)
    emit_w1load(1, te.n)

    relu_done = {}
    mm1_done = {}
    mm2_done = {}
    acc_done = {}
    acc_slice = {}
    xn_loaded = {}
    cw_loaded = {}
    mm2_all = {}
    last_mm2 = [te.n]
    xn_free = [te.n]

    mm1_all = {}
    for si, sub in enumerate(subs):
        kind, widx, half = sub
        w1 = w1slots[si % 2].rearrange("p (c n) -> p c n", c=8)
        w2 = w2slot.rearrange("p (c n) -> p c n", c=16)
        routed = kind == "r"
        bias_e = (4 + widx) if kind == "s" else widx
        hscale = 0.5 if kind == "s" else 1.0
        first_sub = si == 0

        for tc in range(NC2):
            t0 = tc * TC2

            @block.sync
            def _(sy, t0=t0, fa=xn_free[0]):
                sy.wait_ge(te.h, fa)
                sy.dma_start(out=xnbuf[:, :, :], in_=xn_ap[:, :, t0:t0 + TC2]).then_inc(dma0.h, 16)
            dma0.n += 16
            xn_loaded[(si, tc)] = dma0.n

            if routed:
                @block.gpsimd
                def _(g, widx=widx, t0=t0, lm2=last_mm2[0], gev=ge.n):
                    g.wait_ge(ge.h, gev)
                    g.wait_ge(te.h, lm2)
                    g.wait_ge(dma2.h, gate_done_dma)
                    srcap = dt["cw_dram"].ap()
                    ap = bass.AP(tensor=srcap.tensor, offset=widx * NTOK + t0, ap=[[0, 128], [1, TC2]])
                    g.dma_start(out=cwbc, in_=ap).then_inc(dma1.h, 16)
                dma1.n += 16
                cw_loaded[(si, tc)] = dma1.n
            if first_sub:
                @block.gpsimd
                def _(g, t0=t0, lm2=last_mm2[0]):
                    g.wait_ge(dma2.h, gate_done_dma)
                    g.wait_ge(te.h, lm2)
                    g.dma_start(out=sb["cwxsb"][0:4, :],
                                in_=dt["cw_dram"].ap()[0:4, t0:t0 + TC2]).then_inc(dma1.h, 16)
                dma1.n += 16
                cw_loaded[(0, tc)] = dma1.n

            for hs in range(16):
                bank = 2 + (hs % 2)

                @block.tensor
                def _(t, si=si, tc=tc, hs=hs, bank=bank):
                    t.wait_ge(dma0.h, xn_loaded[(si, tc)])
                    t.wait_ge(dma1.h, w1load[si])
                    t.wait_ge(se.h, relu_done.get((bank,), 0))
                    inst = None
                    for c in range(NCH):
                        inst = t.matmul(E.pb(bank), lhsT=w1[:, c, hs * 128:(hs + 1) * 128],
                                        rhs=xnbuf[:, c, :], start=(c == 0), stop=(c == NCH - 1))
                    inst.then_inc(te.h, 1)
                te.n += 1
                mm1_done[(si, tc, hs)] = te.n

                @block.scalar
                def _(s, si=si, tc=tc, hs=hs, bank=bank, bias_e=bias_e, half=half,
                      hscale=hscale, lm2=last_mm2[0], gev=ge.n):
                    s.wait_ge(te.h, mm1_done[(si, tc, hs)])
                    s.wait_ge(ge.h, gev)
                    s.wait_ge(te.h, lm2)
                    s.activation(out=h_sb[:, hs, :], in_=E.pb(bank), func=AF.Relu,
                                 bias=sb["fb1sb"][:, half * 16 + hs, bias_e:bias_e + 1],
                                 scale=1.0).then_inc(se.h, 1)
                se.n += 1
                relu_done[(bank,)] = se.n
                relu_done[(si, tc, hs)] = se.n
            xn_free[0] = mm1_done[(si, tc, 15)]

            if routed:
                @block.gpsimd
                def _(g, si=si, tc=tc):
                    g.wait_ge(dma1.h, cw_loaded[(si, tc)])
                    inst = None
                    for hs in range(16):
                        g.wait_ge(se.h, relu_done[(si, tc, hs)])
                        inst = g.tensor_tensor(out=h_sb[:, hs, :], in0=h_sb[:, hs, :],
                                               in1=cwbc, op=ALU.mult)
                    inst.then_inc(ge.h, 1)
                ge.n += 1
            h_ready = (ge, ge.n) if routed else (se, relu_done[(si, tc, 15)])

            for ds in range(NCH):
                bank = ds % 2

                @block.tensor
                def _(t, si=si, tc=tc, ds=ds, bank=bank, h_ready=h_ready,
                      first_sub=first_sub, hscale=hscale):
                    t.wait_ge(h_ready[0].h, h_ready[1])
                    t.wait_ge(dma1.h, w2load[si])
                    t.wait_ge(ve.h, acc_done.get((bank,), 0))
                    if first_sub:
                        t.wait_ge(dma1.h, cw_loaded[(0, tc)])
                    inst = None
                    for hc in range(16):
                        inst = t.matmul(E.pb(bank), lhsT=w2[:, hc, ds * 128:(ds + 1) * 128],
                                        rhs=h_sb[:, hc, :], start=(hc == 0),
                                        stop=(not first_sub and hc == 15))
                    if first_sub:
                        inst = t.matmul(E.pb(bank), lhsT=sb["eb2msb"][0:6, ds * 128:(ds + 1) * 128],
                                        rhs=sb["cwxsb"][0:6, :], start=False, stop=True)
                    inst.then_inc(te.h, 1)
                te.n += 1
                mm2_done[(si, tc, ds)] = te.n

                @block.vector
                def _(v, si=si, tc=tc, ds=ds, bank=bank, t0=t0, first_sub=first_sub,
                      hscale=hscale):
                    v.wait_ge(te.h, mm2_done[(si, tc, ds)])
                    v.wait_ge(ve.h, acc_slice.get((tc, ds), 0))
                    dst = ACC2[:, ds, t0:t0 + TC2]
                    if first_sub:
                        if hscale != 1.0:
                            v.tensor_scalar(out=dst, in0=E.pb(bank), scalar1=hscale,
                                            scalar2=None, op0=ALU.mult).then_inc(ve.h, 1)
                        else:
                            v.tensor_copy(dst, E.pb(bank)).then_inc(ve.h, 1)
                    else:
                        if hscale != 1.0:
                            v.scalar_tensor_tensor(out=dst, in0=E.pb(bank), scalar=hscale,
                                                   in1=dst, op0=ALU.mult, op1=ALU.add).then_inc(ve.h, 1)
                        else:
                            v.tensor_tensor(out=dst, in0=dst, in1=E.pb(bank),
                                            op=ALU.add).then_inc(ve.h, 1)
                ve.n += 1
                acc_done[(bank,)] = ve.n
                acc_slice[(tc, ds)] = ve.n
            mm2_all[(si, tc)] = te.n
            last_mm2[0] = te.n
            if tc == NC2 - 1:
                mm1_all[si] = mm1_done[(si, tc, 15)]
        if si + 1 < len(subs):
            emit_w2load(si + 1, te.n)
        if si + 2 < len(subs):
            emit_w1load(si + 2, mm1_all[si])

    if "ffn1" in E.dbg:
        dbg_ap = dt["dbg_ffn1"].ap().rearrange("(c p) n -> p c n", p=128)

        @block.sync
        def _(sy, vev=ve.n):
            sy.wait_ge(ve.h, vev)
            sy.dma_start(out=dbg_ap[:, :, :], in_=ACC2[:, :, :]).then_inc(dma2.h, 16)
        dma2.n += 16


def emit_l1attn(E):
    nc, block, dt, sb, sems = E.nc, E.block, E.dt, E.sb, E.sems
    dma0, dma1, dma2 = sems["dma0"], sems["dma1"], sems["dma2"]
    te, ve, se, ge = sems["te"], sems["ve"], sems["se"], sems["ge"]
    onesb, identsb, identf = sb["onesb"], sb["identsb"], sb["identf"]
    xn_ap = dt["xn_dram"].ap().rearrange("(c p) n -> p c n", p=128)
    ACC2 = sb["ACC2"]
    KB = sb["KBUF"].rearrange("p c n -> p (c n)")
    XN2 = sb["XNS"].rearrange("p c n -> p (c n)")
    wk = KB[:, 0:8192].rearrange("p (c n) -> p c n", c=8)
    wv = KB[:, 8192:16384].rearrange("p (c n) -> p c n", c=8)
    wr = XN2[:, 0:8192].rearrange("p (c n) -> p c n", c=8)
    wo = XN2[:, 8192:16384].rearrange("p (c n) -> p c n", c=8)
    WRK, WF, VB = sb["WRK"], sb["WRKF"], sb["VBUF"]
    WRKf = sb["WRK"].bitcast(F32)
    VBf = sb["VBUF"].bitcast(F32)
    sA = WRK[:, 0:1024]
    sB = WRK[:, 1024:2048]
    sC = WRK[:, 2048:3072]
    sD = WRK[:, 3072:4096]
    sE = WRK[:, 4096:5120]
    sF = WRK[:, 5120:6144]
    sG = WRK[:, 6144:7168]
    sH = WRK[:, 7168:8192]
    v0 = VB[:, 0:1024]
    v1 = VB[:, 1024:2048]
    v2 = VB[:, 2048:3072]
    v3 = VB[:, 3072:4096]
    v4 = VB[:, 4096:5120]
    v5 = VB[:, 5120:6144]
    v6 = VB[:, 6144:7168]
    v7 = VB[:, 7168:8192]
    ef1 = VBf[:, 3072:4096]
    ef2 = WRKf[:, 2048:3072]
    xld = VB[:, 8192:9224].rearrange("p (c n) -> p c n", c=8)
    hg5 = VB[0:32, 9224:9874].rearrange("p (sg j r) -> p sg j r", sg=5, r=5)
    hgflat = VB[0:32, 9224:9874]
    hgstage = VB[0:32, 9874:10004]
    w_fm = WF[:, 0:1024]
    cw = WF[:, 1024:2048]
    cc_in, cc_out, hl = dt["cc_in"], dt["cc_out"], dt["hl_dram"]
    lrowf = sb["qkvob"].bitcast(F32)
    sb["wg1sb"] = VB[:, 10004:11284].rearrange("p (c n) -> p c n", n=160)
    sb["wg2a"] = VB[:, 11284:12308]
    sb["Bpbuf"] = VB[0:32, 12308:13332]
    sb["tdBsb"] = VB[64:128, 12308:13332]
    sb["wg2b"] = VB[0:32, 13332:14356]
    sb["tdAsb"] = VB[:, 14356:14868].rearrange("p (c n) -> p c n", c=8)
    sb["Ssb"] = VB[:, 14868:15380].rearrange("p (c n) -> p c n", c=8)
    sb["Zsb"] = VB[:, 15380:15892].rearrange("p (c n) -> p c n", c=8)
    sb["A1sb"] = VB[:, 5120:6400].rearrange("p (c n) -> p c n", n=160)
    RF = sb["recipbc"].bitcast(F32).rearrange("p a b -> p (a b)")
    Rb = sb["recipbc"].rearrange("p a b -> p (a b)")
    sb["ecols"] = RF[:, 0:72].rearrange("p (a b) -> p a b", b=9)
    sb["ecole"] = RF[:, 72:144].rearrange("p (a b) -> p a b", b=9)
    sb["onesf128"] = RF[:, 144:272]
    sb["mrsb"] = Rb[0:16, 544:672]
    sb["rssb"] = Rb[0:16, 672:800]
    Fb = sb["fb1sb"].rearrange("p a b -> p (a b)")
    sb["pcols"] = Fb[:, 0:80].rearrange("p (c k) -> p c k", k=10)
    sb["parsb"] = Fb[:, 80:84]
    VBF = sb["VBUF"].bitcast(F32)
    sb["iot0"] = VBF[0:1, 7946:8074]
    sb["iot1"] = VBF[0:1, 8074:8202]
    sb["INDsb"] = sb["eb2msb"][:, :]
    sb["maskTsb"] = sb["rmasksb"][:, 0:128]
    sb["Mkhatsb"] = sb["rmasksb"][:, 128:256]
    sb["Mkbrvsb"] = sb["bwmasksb"][:, 0:128]
    pcols = sb["pcols"]

    def fmh(t, h):
        return t.rearrange("p (c n) -> p c n", c=8)[(h % 2) * 64:(h % 2) * 64 + 64, h // 2, :]

    E.V = dict(sA=sA, sB=sB, sC=sC, sD=sD, sE=sE, sF=sF, sG=sG, sH=sH,
               v0=v0, v1=v1, v2=v2, v3=v3, v4=v4, v5=v5, v6=v6, v7=v7,
               ef1=ef1, ef2=ef2, xld=xld, hg5=hg5, hgflat=hgflat, hgstage=hgstage,
               w_fm=w_fm, cw=cw,
               WF=WF, wk=wk, wv=wv, wr=wr, wo=wo, lrowf=lrowf, fmh=fmh)

    if L1_PHASEA < 0:
        return

    NPRO = 28

    @block.gpsimd
    def _(g, tev=te.n, vev=ve.n, sev=se.n, d1=dma1.n):
        g.wait_ge(te.h, tev)
        g.wait_ge(ve.h, vev)
        g.wait_ge(se.h, sev)
        g.wait_ge(dma1.h, d1)
        loads = []
        for wname, dst in [("l1wkT", wk), ("l1wvT", wv), ("l1wrT", wr), ("l1woT", wo)]:
            loads.append((dst, dt[wname].ap().rearrange("(c p) n -> p c n", p=128), False))
        loads.append((sb["A1sb"][:, :, :], dt["A1"].ap().rearrange("(c p) n -> p c n", p=128), False))
        loads.append((sb["tdAsb"][:, :, :], dt["tdA"].ap().rearrange("(c p) n -> p c n", p=128), False))
        loads.append((sb["tdBsb"][0:64, :], dt["tdB"].ap(), False))
        loads.append((sb["wg1sb"][:, :, :], dt["Wg1"].ap().rearrange("(c p) n -> p c n", p=128), False))
        loads.append((sb["wg2a"][:, :], dt["Wg2"].ap()[0:128, :], False))
        loads.append((sb["wg2b"][0:32, :], dt["Wg2"].ap()[128:160, :], False))
        for name, c0 in [("miu_x", 0), ("td_miu", 6), ("u1", 7), ("gamma", 8), ("beta", 9)]:
            loads.append((pcols[:, :, c0:c0 + 1],
                          dt[name].ap().rearrange("(c p) -> p c", p=128).unsqueeze(2), True))
        for sgi in range(5):
            loads.append((pcols[:, :, 1 + sgi:2 + sgi],
                          dt["lam"].ap()[sgi].rearrange("(c p) -> p c", p=128).unsqueeze(2), True))
        loads.append((lrowf[0:1, 0:1024], dt["td_miu"].ap().unsqueeze(0), False))
        loads.append((sb["iot0"][:, :], dt["l1rows"].ap()[0:1, :], False))
        loads.append((sb["iot1"][:, :], dt["l1rows"].ap()[1:2, :], False))
        loads.append((sb["parsb"][:, :], dt["l1cons"].ap(), False))
        loads.append((sb["maskTsb"][:, :], dt["maskT"].ap(), False))
        loads.append((sb["Mkhatsb"][:, :], dt["Mkhat_c"].ap(), False))
        loads.append((sb["Mkbrvsb"][:, :], dt["Mkbrv_c"].ap(), False))
        loads.append((sb["INDsb"][:, :], dt["INDf"].ap(), False))
        assert len(loads) == NPRO, len(loads)
        for i, (o, inp, slow) in enumerate(loads):
            if i < L1_PRON:
                g.dma_start(out=o, in_=inp, allow_slow_non_contiguous=slow).then_inc(dma1.h, 16)
            else:
                g.dma_start(out=sb["cwbc1"][:, 0:2], in_=dt["identity"].ap()[:, 0:2],
                            allow_slow_non_contiguous=True).then_inc(dma1.h, 16)
        g.memset(sb["onesf128"][:, :], 1.0)
        g.engine_nop().then_inc(ge.h, 1)
    dma1.n += 28 * 16
    ge.n += 1
    pro_ld = dma1.n
    if L1_PHASEA < 1:
        return

    xldA = WRK[:, 0:4104].rearrange("p (c n) -> p c n", c=8)
    xxv = VB[:, 0:4096].rearrange("p (c n) -> p c n", c=8)
    hb1 = WRK[:, 4104:4616]
    hb2 = WRK[0:32, 4616:5128]
    for half in range(8):
        s, ck = divmod(half, 4)
        t0 = s * T + ck * 512

        @block.sync
        def _(sy, t0=t0, ck=ck, tev=te.n, dv=E.xn_written):
            sy.wait_ge(dma0.h, dv)
            sy.wait_ge(te.h, tev)
            if ck == 0:
                sy.dma_start(out=xldA[:, :, 1:513], in_=xn_ap[:, :, t0:t0 + 512]).then_inc(dma0.h, 16)
            else:
                sy.dma_start(out=xldA[:, :, :], in_=xn_ap[:, :, t0 - 1:t0 + 512]).then_inc(dma0.h, 16)
        dma0.n += 16
        if ck == 0:
            @block.gpsimd
            def _(g, vev=ve.n):
                g.wait_ge(ve.h, vev)
                g.memset(xldA[:, :, 0:1], 0.0)
                g.engine_nop().then_inc(ge.h, 1)
            ge.n += 1
        xa_ld = dma0.n
        gev0 = ge.n

        @block.vector
        def _(v, xa_ld=xa_ld, gev0=gev0, tev=te.n):
            v.wait_ge(dma0.h, xa_ld)
            v.wait_ge(ge.h, gev0)
            v.wait_ge(dma1.h, pro_ld)
            v.wait_ge(te.h, tev)

            def step(inst):
                ve.inc(inst)
                v.wait_ge(ve.h, ve.n)
            for c in range(8):
                step(v.tensor_tensor(out=xxv[:, c, :], in0=xldA[:, c, 0:512],
                                     in1=xldA[:, c, 1:513], op=ALU.subtract))
                step(v.scalar_tensor_tensor(out=xxv[:, c, :], in0=xxv[:, c, :],
                                            scalar=pcols[:, c, 0:1], in1=xldA[:, c, 1:513],
                                            op0=ALU.mult, op1=ALU.add))
        xx_done = ve.n

        @block.tensor
        def _(t, xx_done=xx_done, sev=se.n):
            t.wait_ge(ve.h, xx_done)
            t.wait_ge(dma1.h, pro_ld)
            t.wait_ge(se.h, sev)
            inst = None
            for c in range(8):
                inst = t.matmul(E.pb(2), lhsT=sb["A1sb"][:, c, 0:128], rhs=xxv[:, c, :],
                                start=(c == 0), stop=(c == 7))
            for c in range(8):
                inst = t.matmul(E.pb(3)[0:32, :], lhsT=sb["A1sb"][:, c, 128:160],
                                rhs=xxv[:, c, :], start=(c == 0), stop=(c == 7))
            inst.then_inc(te.h, 1)
        te.n += 1
        hmm_done = te.n

        @block.scalar
        def _(s_, hmm_done=hmm_done, dv=dma0.n):
            s_.wait_ge(te.h, hmm_done)
            s_.wait_ge(dma0.h, dv)
            s_.activation(out=hb1, in_=E.pb(2), func=AF.Tanh).then_inc(se.h, 1)
            s_.activation(out=hb2, in_=E.pb(3)[0:32, :], func=AF.Tanh).then_inc(se.h, 1)
        se.n += 2
        tanh_done = se.n

        @block.sync
        def _(sy, tanh_done=tanh_done, t0=t0):
            sy.wait_ge(se.h, tanh_done)
            sy.dma_start(out=cc_in.ap()[0:128, t0:t0 + 512], in_=hb1).then_inc(dma0.h, 16)
            sy.dma_start(out=cc_in.ap()[128:160, t0:t0 + 512], in_=hb2).then_inc(dma0.h, 16)
        dma0.n += 32
    h_written = dma0.n

    if L1_PHASEA < 2:
        return

    @block.gpsimd
    def _(g, h_written=h_written):
        g.wait_ge(dma0.h, h_written)
        g.collective_compute("AllGather", ALU.bypass, replica_groups=[list(range(8))],
                             ins=[cc_in.ap().opt()],
                             outs=[cc_out.ap().opt()]).then_inc(ge.h, 1)
    ge.n += 1
    cc_done = ge.n

    if L1_PHASEA < 3:
        return
    NSEG = 52

    @block.sync
    def _(sy, cc_done=cc_done):
        sy.wait_ge(ge.h, cc_done)
        for arm in sy.switch_core_id(n=128):
            c = arm.logical % 8
            segs = []
            for sg in range(5):
                for V in range(5):
                    Mb = 32768 * sg + 4096 * c
                    cls = (V - Mb) % 5
                    n_u = len(range(cls, 4096, 5))
                    g0 = (Mb + cls) // 5
                    gpos, wpos, left = g0, 0, n_u
                    while left > 0:
                        bnd = ((gpos // 4096) + 1) * 4096
                        ln = min(left, bnd - gpos)
                        segs.append((sg, V, cls, gpos, wpos, ln))
                        wpos += ln
                        gpos += ln
                        left -= ln
            assert len(segs) <= NSEG, len(segs)
            for (sg, V, cls, gpos, wpos, ln) in segs:
                r = gpos // 4096
                src = bass.AP(tensor=cc_out.ap().tensor,
                              offset=(160 * r + 32 * V) * NTOK + (gpos % 4096),
                              ap=[[NTOK, 32], [1, ln]])
                dst = bass.AP(tensor=hl.ap().tensor,
                              offset=(sg * 32 * 5 + cls) * 820 + wpos,
                              ap=[[5 * 820, 32], [1, ln]])
                sy.dma_start(out=dst, in_=src).then_inc(dma2.h, 16)
            for _ in range(NSEG - len(segs)):
                sy.dma_start(out=bass.AP(tensor=dt["scr_dn"].ap().tensor, offset=0,
                                         ap=[[1, 1], [1, 1]]),
                             in_=bass.AP(tensor=cc_out.ap().tensor, offset=0,
                                         ap=[[1, 1], [1, 1]])).then_inc(dma2.h, 16)
    dma2.n += NSEG * 16
    hl_ready = dma2.n

    if "l1w" in E.dbg:
        @block.sync
        def _(sy, hr2=hl_ready):
            sy.wait_ge(dma2.h, hr2)
            sy.dma_start(out=dt["dbg_hl"].ap()[:, :, :, :],
                         in_=hl.ap()[:, :, :, :]).then_inc(dma2.h, 16)
            sy.dma_start(out=dt["dbg_cc"].ap()[:, :], in_=cc_in.ap()[:, :]).then_inc(dma2.h, 16)
        dma2.n += 32

    for s in range(BPC):
        @block.gpsimd
        def _(g, tev=te.n, vev=ve.n):
            g.wait_ge(te.h, tev)
            g.wait_ge(ve.h, vev)
            g.memset(sb["Ssb"][:, :, :], 0.0)
            g.engine_nop().then_inc(ge.h, 1)
        ge.n += 1
        s_reset = ge.n

        for tcl in range(min(T // 128, L1_TCMAX - s * 16)):
            t0 = s * T + tcl * 128

            TCg = s * 16 + tcl
            rci = []
            for rho in range(5):
                rci.append(((3 * TCg + rho) % 5, (128 * TCg + rho) // 5,
                            26 if rho < 3 else 25))

            @block.sync
            def _(sy, t0=t0, tcl=tcl, tev=te.n, vev=ve.n, dv=E.xn_written, hr2=hl_ready):
                sy.wait_ge(dma0.h, dv)
                sy.wait_ge(dma2.h, hr2)
                sy.wait_ge(te.h, tev)
                sy.wait_ge(ve.h, vev)
                if tcl == 0:
                    sy.dma_start(out=xld[:, :, 1:129], in_=xn_ap[:, :, t0:t0 + 128]).then_inc(dma0.h, 16)
                else:
                    sy.dma_start(out=xld[:, :, :], in_=xn_ap[:, :, t0 - 1:t0 + 128]).then_inc(dma0.h, 16)
            dma0.n += 16
            if tcl == 0:
                @block.gpsimd
                def _(g, vev=ve.n):
                    g.wait_ge(ve.h, vev)
                    g.memset(xld[:, :, 0:1], 0.0)
                    g.engine_nop().then_inc(ge.h, 1)
                ge.n += 1
            ld_done = dma0.n
            gev1 = ge.n

            @block.vector
            def _(v, ld_done=ld_done, gev1=gev1, tev=te.n):
                v.wait_ge(dma0.h, ld_done)
                v.wait_ge(ge.h, gev1)
                v.wait_ge(te.h, tev)
                inst = None
                for c in range(8):
                    inst = v.tensor_tensor(out=sA.rearrange("p (c n) -> p c n", c=8)[:, c, :],
                                           in0=xld[:, c, 0:128], in1=xld[:, c, 1:129],
                                           op=ALU.subtract)
                inst.then_inc(ve.h, 1)
            ve.n += 1
            ba_done = ve.n

            ecol_done = [0]
            facts_done = [0]
            prod_done = [0]
            ksplit_done = [0]

            for sg in range(5):
                @block.sync
                def _(sy, sg=sg, gev=ge.n, hr2=hl_ready):
                    sy.wait_ge(ge.h, gev)
                    sy.wait_ge(dma2.h, hr2)
                    for rho, (cls, w0, cnt) in enumerate(rci):
                        src = bass.AP(tensor=hl.ap().tensor,
                                      offset=(sg * 32 * 5 + cls) * 820 + w0,
                                      ap=[[5 * 820, 32], [1, cnt]])
                        sy.dma_start(out=hgstage[0:32, rho * 26:rho * 26 + cnt],
                                     in_=src).then_inc(dma0.h, 16)
                dma0.n += 5 * 16
                hg_ld = dma0.n

                @block.gpsimd
                def _(g, sg=sg, hg_ld=hg_ld, tev=te.n):
                    g.wait_ge(dma0.h, hg_ld)
                    g.wait_ge(te.h, tev)
                    inst = None
                    for rho, (cls, w0, cnt) in enumerate(rci):
                        inst = g.tensor_copy(hg5[0:32, sg, 0:cnt, rho],
                                             hgstage[0:32, rho * 26:rho * 26 + cnt])
                    inst.then_inc(ge.h, 1)
                ge.n += 1
                hg_cp = ge.n

                @block.gpsimd
                def _(g, sg=sg, tev=te.n):
                    g.wait_ge(te.h, tev)
                    g.dma_start(out=sb["Bpbuf"][0:32, :],
                                in_=dt["Bp"].ap()[sg]).then_inc(dma1.h, 16)
                dma1.n += 16
                bp_ld = dma1.n

                @block.tensor
                def _(t, sg=sg, hg_cp=hg_cp, vev=ve.n, bp_ld=bp_ld):
                    t.wait_ge(ge.h, hg_cp)
                    t.wait_ge(dma1.h, bp_ld)
                    t.wait_ge(ve.h, vev)
                    inst = None
                    for dc in range(8):
                        pbq = E.pb(2, (dc % 4) * 128, (dc % 4) * 128 + 128) if dc < 4 \
                            else E.pb(3, (dc % 4) * 128, (dc % 4) * 128 + 128)
                        inst = t.matmul(pbq, lhsT=sb["Bpbuf"][0:32, dc * 128:dc * 128 + 128],
                                        rhs=hgflat[0:32, sg * 130:sg * 130 + 128],
                                        start=True, stop=True)
                    inst.then_inc(te.h, 1)
                te.n += 1
                miu_done = te.n

                @block.vector
                def _(v, sg=sg, miu_done=miu_done, ba_done=ba_done):
                    v.wait_ge(te.h, miu_done)
                    v.wait_ge(ve.h, ba_done)

                    def step(inst):
                        ve.inc(inst)
                        v.wait_ge(ve.h, ve.n)
                    x5v = sB.rearrange("p (c n) -> p c n", c=8)
                    bav = sA.rearrange("p (c n) -> p c n", c=8)
                    for dc in range(8):
                        pbq = E.pb(2, (dc % 4) * 128, (dc % 4) * 128 + 128) if dc < 4 \
                            else E.pb(3, (dc % 4) * 128, (dc % 4) * 128 + 128)
                        step(v.scalar_tensor_tensor(out=x5v[:, dc, :], in0=pbq,
                                                    scalar=pcols[:, dc, 1 + sg:2 + sg],
                                                    in1=bav[:, dc, :],
                                                    op0=ALU.add, op1=ALU.mult))
                        step(v.tensor_tensor(out=x5v[:, dc, :], in0=x5v[:, dc, :],
                                             in1=xld[:, dc, 1:129], op=ALU.add))
                ve.n += 0
                x5_done = ve.n

                if sg == 0:
                    if L1_PARTS >= 1:
                        emit_sg0(E, x5_done, t0)
                        ecol_done[0] = E._sg0_ecole
                elif sg == 1:
                    if L1_PARTS >= 2:
                        emit_sg1(E, x5_done)
                elif sg == 2:
                    if L1_PARTS >= 2:
                        emit_sg2(E, x5_done)
                elif sg == 3:
                    if L1_PARTS >= 2:
                        emit_sg3(E, x5_done)
                        facts_done[0] = E._sg3_facts
                        prod_done[0] = E._sg3_prod
                elif sg == 4:
                    if L1_PARTS >= 2:
                        emit_sg4(E, x5_done)
                        ksplit_done[0] = E._sg4_ksplit
                if L1_PARTS < 2 and sg > 0:
                    break

            if L1_PARTS >= 4:
                emit_headloop(E, t0, s_reset, ecol_done[0], facts_done[0], prod_done[0],
                              ksplit_done[0])
                emit_gn_wo(E, t0)

            if "l1w" in E.dbg and s == 0 and tcl == 0:
                @block.sync
                def _(sy, tev=te.n, vev=ve.n, sev=se.n):
                    sy.wait_ge(te.h, tev)
                    sy.wait_ge(ve.h, vev)
                    sy.wait_ge(se.h, sev)
                    sy.dma_start(out=dt["dbg_wrk1"].ap()[:, :], in_=WRK[:, :]).then_inc(dma2.h, 16)
                    sy.dma_start(out=dt["dbg_wrkf1"].ap()[:, :], in_=WF[:, :]).then_inc(dma2.h, 16)
                    sy.dma_start(out=dt["dbg_vb1"].ap()[:, :], in_=VB[:, 0:8192]).then_inc(dma2.h, 16)
                dma2.n += 48
                nxt = dma2.n

                @block.vector
                def _(v, nxt=nxt):
                    v.wait_ge(dma2.h, nxt)
                    nc.vector.engine_nop().then_inc(ve.h, 1)
                ve.n += 1

    if "l1sum" in E.dbg:
        dbg_ap = dt["dbg_l1sum"].ap().rearrange("(c p) n -> p c n", p=128)

        @block.sync
        def _(sy, vev=ve.n):
            sy.wait_ge(ve.h, vev)
            sy.dma_start(out=dbg_ap[:, :, :], in_=ACC2[:, :, :]).then_inc(dma2.h, 16)
        dma2.n += 16


def _quarter(E, dc):
    b = 2 if dc < 4 else 3
    q = (dc % 4) * 128
    return E.pb(b, q, q + 128)


def emit_sg0(E, x5_done, t0):
    nc, block, sb, sems = E.nc, E.block, E.sb, E.sems
    te, ve, se = sems["te"], sems["ve"], sems["se"]
    V = E.V
    sC, sD, sE, sF, v6 = V["sC"], V["sD"], V["sE"], V["sF"], V["v6"]
    w_fm, cw, WF = V["w_fm"], V["cw"], V["WF"]
    tw = sb["cwbc1"][64:128, 0:128]

    @block.tensor
    def _(t, x5_done=x5_done, sev=se.n):
        t.wait_ge(ve.h, x5_done)
        t.wait_ge(se.h, sev)
        inst = None
        for c in range(8):
            inst = t.matmul(E.pb(4, 0, 128)[0:64, :], lhsT=sb["tdAsb"][:, c, :],
                            rhs=V["sB"].rearrange("p (c n) -> p c n", c=8)[:, c, :],
                            start=(c == 0), stop=(c == 7))
        inst.then_inc(te.h, 1)
    te.n += 1
    tdamm = te.n

    @block.scalar
    def _(s_, tdamm=tdamm):
        s_.wait_ge(te.h, tdamm)
        s_.activation(out=tw, in_=E.pb(4, 0, 128)[0:64, :], func=AF.Tanh).then_inc(se.h, 1)
    se.n += 1
    tw_done = se.n

    @block.tensor
    def _(t, tw_done=tw_done, x5_done=x5_done):
        t.wait_ge(se.h, tw_done)
        t.wait_ge(ve.h, x5_done)
        inst = None
        for dc in range(8):
            inst = t.matmul(_quarter(E, dc), lhsT=sb["tdBsb"][0:64, dc * 128:dc * 128 + 128],
                            rhs=tw, start=True, stop=True)
        inst.then_inc(te.h, 1)
    te.n += 1
    dfm = te.n

    @block.vector
    def _(v, dfm=dfm):
        v.wait_ge(te.h, dfm)
        inst = None
        for dc in range(8):
            inst = v.tensor_scalar(out=w_fm[:, dc * 128:dc * 128 + 128], in0=_quarter(E, dc),
                                   scalar1=sb["pcols"][:, dc, 6:7], scalar2=None, op0=ALU.add)
        inst.then_inc(ve.h, 1)
    ve.n += 1
    wfm_done = ve.n

    @block.tensor
    def _(t, tw_done=tw_done, x5_done=x5_done):
        t.wait_ge(se.h, tw_done)
        t.wait_ge(ve.h, x5_done)
        t.matmul(E.pb(5), lhsT=tw, rhs=sb["tdBsb"][0:64, 0:512], start=True, stop=False)
        t.matmul(E.pb(6), lhsT=tw, rhs=sb["tdBsb"][0:64, 512:1024],
                 start=True, stop=False).then_inc(te.h, 1)
    te.n += 1
    dtp = te.n

    @block.vector
    def _(v, dtp=dtp):
        v.wait_ge(te.h, dtp)
        v.tensor_copy(sD[:, 0:512], E.pb(5))
        v.tensor_copy(sD[:, 512:1024], E.pb(6)).then_inc(ve.h, 1)
    ve.n += 1
    devac = ve.n

    @block.tensor
    def _(t, devac=devac):
        t.wait_ge(ve.h, devac)
        t.matmul(E.pb(5), lhsT=sb["onesf128"][0:1, :], rhs=V["lrowf"][0:1, 0:512],
                 start=False, stop=True)
        t.matmul(E.pb(6), lhsT=sb["onesf128"][0:1, :], rhs=V["lrowf"][0:1, 512:1024],
                 start=False, stop=True).then_inc(te.h, 1)
    te.n += 1
    wtp = te.n

    @block.vector
    def _(v, wtp=wtp):
        v.wait_ge(te.h, wtp)
        v.tensor_scalar(out=v6[:, 0:512], in0=E.pb(5), scalar1=0.0, scalar2=None, op0=ALU.min)
        v.tensor_scalar(out=v6[:, 512:1024], in0=E.pb(6), scalar1=0.0, scalar2=None,
                        op0=ALU.min).then_inc(ve.h, 1)
    ve.n += 1
    wmin_done = ve.n

    @block.scalar
    def _(s_, wmin_done=wmin_done):
        s_.wait_ge(ve.h, wmin_done)
        s_.activation(out=sC[:, :], in_=v6[:, :], func=AF.Exp).then_inc(se.h, 1)
    se.n += 1
    gatetp_done = se.n
    E._sg0_gate = gatetp_done

    @block.tensor
    def _(t, gatetp_done=gatetp_done, wfm_done=wfm_done):
        t.wait_ge(se.h, gatetp_done)
        t.wait_ge(ve.h, wfm_done)
        t.matmul(E.pb(5), lhsT=sb["Mkhatsb"][:, :], rhs=sD[:, 0:512], start=True, stop=False)
        t.matmul(E.pb(5), lhsT=sb["iot0"][0:1, :], rhs=V["lrowf"][0:1, 0:512],
                 start=False, stop=True)
        t.matmul(E.pb(6), lhsT=sb["Mkhatsb"][:, :], rhs=sD[:, 512:1024], start=True, stop=False)
        t.matmul(E.pb(6), lhsT=sb["iot0"][0:1, :], rhs=V["lrowf"][0:1, 512:1024],
                 start=False, stop=True)
        t.matmul(E.pb(2), lhsT=sb["Mkbrvsb"][:, :], rhs=sD[:, 0:512], start=True, stop=False)
        t.matmul(E.pb(2), lhsT=sb["iot1"][0:1, :], rhs=V["lrowf"][0:1, 0:512],
                 start=False, stop=True)
        t.matmul(E.pb(3), lhsT=sb["Mkbrvsb"][:, :], rhs=sD[:, 512:1024], start=True, stop=False)
        t.matmul(E.pb(3), lhsT=sb["iot1"][0:1, :], rhs=V["lrowf"][0:1, 512:1024],
                 start=False, stop=True).then_inc(te.h, 1)
    te.n += 1
    expmm = te.n

    @block.scalar
    def _(s_, expmm=expmm):
        s_.wait_ge(te.h, expmm)
        s_.activation(out=sE[:, 0:512], in_=E.pb(5), func=AF.Exp)
        s_.activation(out=sE[:, 512:1024], in_=E.pb(6), func=AF.Exp)
        s_.activation(out=sF[:, 0:512], in_=E.pb(2), func=AF.Exp)
        s_.activation(out=sF[:, 512:1024], in_=E.pb(3), func=AF.Exp).then_inc(se.h, 1)
    se.n += 1
    E._sg0_expf = se.n

    @block.vector
    def _(v, wfm_done=wfm_done):
        v.wait_ge(ve.h, wfm_done)

        def step(inst):
            ve.inc(inst)
            v.wait_ge(ve.h, ve.n)
        for dc in range(8):
            step(v.tensor_tensor_scan(out=cw[:, dc * 128:dc * 128 + 128],
                                      data0=sb["onesf128"][:, :],
                                      data1=w_fm[:, dc * 128:dc * 128 + 128],
                                      initial=0.0, op0=ALU.mult, op1=ALU.add))
        for dc in range(8):
            base = 1024 + dc * 128
            for ss in range(8):
                hi = base + 127 if ss == 7 else base + 16 * ss + 23
                step(v.tensor_tensor(out=sb["ecols"][:, dc, ss:ss + 1],
                                     in0=WF[:, hi:hi + 1],
                                     in1=WF[:, base + 16 * ss + 7:base + 16 * ss + 8],
                                     op=ALU.subtract))
    scan_done = ve.n

    @block.scalar
    def _(s_, scan_done=scan_done):
        s_.wait_ge(ve.h, scan_done)
        inst = None
        for dc in range(8):
            inst = s_.activation(out=sb["ecole"][:, dc, 0:8], in_=sb["ecols"][:, dc, 0:8],
                                 func=AF.Exp)
            inst = s_.activation(out=sb["ecole"][:, dc, 8:9],
                                 in_=WF[:, 1024 + dc * 128 + 127:1024 + dc * 128 + 128],
                                 func=AF.Exp)
        inst.then_inc(se.h, 1)
    se.n += 1
    E._sg0_ecole = se.n


def emit_sg1(E, x5_done):
    nc, block, sb, sems = E.nc, E.block, E.sb, E.sems
    te, ve, se = sems["te"], sems["ve"], sems["se"]
    V = E.V
    sC, sD, sE, sF, sG, v6, v7 = V["sC"], V["sD"], V["sE"], V["sF"], V["sG"], V["v6"], V["v7"]
    w_fm, cw, WF, wk = V["w_fm"], V["cw"], V["WF"], V["wk"]
    x5v = V["sB"].rearrange("p (c n) -> p c n", c=8)

    @block.tensor
    def _(t, x5_done=x5_done, sev=se.n):
        t.wait_ge(ve.h, x5_done)
        t.wait_ge(se.h, sev)
        inst = None
        for dc in range(8):
            for c in range(8):
                inst = t.matmul(_quarter(E, dc), lhsT=wk[:, c, dc * 128:dc * 128 + 128],
                                rhs=x5v[:, c, :], start=(c == 0), stop=(c == 7))
        inst.then_inc(te.h, 1)
    te.n += 1
    kfm = te.n

    @block.vector
    def _(v, gt=E._sg0_gate):
        v.wait_ge(se.h, gt)
        v.tensor_scalar(out=v6[:, :], in0=w_fm[:, :], scalar1=0.0, scalar2=None,
                        op0=ALU.min).then_inc(ve.h, 1)
    ve.n += 1
    gmin = ve.n

    @block.scalar
    def _(s_, gmin=gmin):
        s_.wait_ge(ve.h, gmin)
        s_.activation(out=v7[:, :], in_=v6[:, :], func=AF.Exp).then_inc(se.h, 1)
    se.n += 1
    gfm_done = se.n

    @block.vector
    def _(v, kfm=kfm, gfm_done=gfm_done):
        v.wait_ge(te.h, kfm)
        v.wait_ge(se.h, gfm_done)
        inst = None
        for dc in range(8):
            inst = v.tensor_tensor(out=sG[:, dc * 128:dc * 128 + 128], in0=_quarter(E, dc),
                                   in1=v7[:, dc * 128:dc * 128 + 128], op=ALU.mult)
        inst.then_inc(ve.h, 1)
    ve.n += 1
    kbar_done = ve.n

    @block.tensor
    def _(t, x5_done=x5_done, ef=E._sg0_expf):
        t.wait_ge(ve.h, x5_done)
        t.wait_ge(se.h, ef)
        inst = None
        for c in range(8):
            inst = t.matmul(E.pb(5), lhsT=x5v[:, c, :], rhs=wk[:, c, 0:512],
                            start=(c == 0), stop=(c == 7))
        for c in range(8):
            inst = t.matmul(E.pb(6), lhsT=x5v[:, c, :], rhs=wk[:, c, 512:1024],
                            start=(c == 0), stop=(c == 7))
        inst.then_inc(te.h, 1)
    te.n += 1
    ktp = te.n

    @block.vector
    def _(v, ktp=ktp, gf=gfm_done):
        v.wait_ge(te.h, ktp)
        v.wait_ge(se.h, gf)

        def step(inst):
            ve.inc(inst)
            v.wait_ge(ve.h, ve.n)
        step(v.tensor_tensor(out=v6[:, 0:512], in0=E.pb(5), in1=sC[:, 0:512], op=ALU.mult))
        step(v.tensor_tensor(out=v6[:, 512:1024], in0=E.pb(6), in1=sC[:, 512:1024], op=ALU.mult))
        step(v.tensor_tensor(out=sD[:, :], in0=v6[:, :], in1=sE[:, :], op=ALU.mult))
        step(v.tensor_tensor(out=sC[:, :], in0=v6[:, :], in1=sF[:, :], op=ALU.mult))
        for dc in range(8):
            step(v.tensor_copy(WF[:, dc * 128 + 1:dc * 128 + 128],
                               cw[:, dc * 128:dc * 128 + 127]))
            step(v.tensor_scalar(out=WF[:, dc * 128:dc * 128 + 1], in0=WF[:, 1024:1025],
                                 scalar1=0.0, scalar2=None, op0=ALU.mult))
    E._sg1_done = ve.n


def emit_sg2(E, x5_done):
    nc, block, sb, sems = E.nc, E.block, E.sb, E.sems
    te, ve, se = sems["te"], sems["ve"], sems["se"]
    V = E.V
    sH, wv = V["sH"], V["wv"]
    x5v = V["sB"].rearrange("p (c n) -> p c n", c=8)

    @block.tensor
    def _(t, x5_done=x5_done, kg=E._sg1_done):
        t.wait_ge(ve.h, x5_done)
        t.wait_ge(ve.h, kg)
        inst = None
        for c in range(8):
            inst = t.matmul(E.pb(5), lhsT=x5v[:, c, :], rhs=wv[:, c, 0:512],
                            start=(c == 0), stop=(c == 7))
        for c in range(8):
            inst = t.matmul(E.pb(6), lhsT=x5v[:, c, :], rhs=wv[:, c, 512:1024],
                            start=(c == 0), stop=(c == 7))
        inst.then_inc(te.h, 1)
    te.n += 1
    vtp = te.n

    @block.vector
    def _(v, vtp=vtp):
        v.wait_ge(te.h, vtp)
        v.tensor_copy(sH[:, 0:512], E.pb(5))
        v.tensor_copy(sH[:, 512:1024], E.pb(6)).then_inc(ve.h, 1)
    ve.n += 1
    E._sg2_done = ve.n


def emit_sg3(E, x5_done):
    nc, block, sb, sems = E.nc, E.block, E.sb, E.sems
    te, ve, se = sems["te"], sems["ve"], sems["se"]
    V = E.V
    sG, v0, v2, v3, v4 = V["sG"], V["v0"], V["v2"], V["v3"], V["v4"]
    ef1, ef2, cw, WF, wr = V["ef1"], V["ef2"], V["cw"], V["WF"], V["wr"]
    x5v = V["sB"].rearrange("p (c n) -> p c n", c=8)

    @block.tensor
    def _(t, x5_done=x5_done):
        t.wait_ge(ve.h, x5_done)
        inst = None
        for dc in range(8):
            for c in range(8):
                inst = t.matmul(_quarter(E, dc), lhsT=wr[:, c, dc * 128:dc * 128 + 128],
                                rhs=x5v[:, c, :], start=(c == 0), stop=(c == 7))
        inst.then_inc(te.h, 1)
    te.n += 1
    rfm = te.n

    @block.vector
    def _(v, rfm=rfm):
        v.wait_ge(te.h, rfm)

        def step(inst):
            ve.inc(inst)
            v.wait_ge(ve.h, ve.n)
        for dc in range(8):
            step(v.tensor_copy(v0[:, dc * 128:dc * 128 + 128], _quarter(E, dc)))
        for dc in range(8):
            for ss in range(8):
                lo = dc * 128 + 16 * ss
                cc = 1024 + dc * 128 + 16 * ss + 7
                step(v.tensor_scalar(out=ef1[:, lo:lo + 16], in0=WF[:, lo:lo + 16],
                                     scalar1=WF[:, cc:cc + 1], scalar2=None,
                                     op0=ALU.subtract))
                step(v.tensor_scalar(out=ef2[:, lo:lo + 16], in0=WF[:, 1024 + lo:1024 + lo + 16],
                                     scalar1=WF[:, cc:cc + 1], scalar2=None,
                                     op0=ALU.subtract))
    ef_done = ve.n

    @block.scalar
    def _(s_, ef_done=ef_done):
        s_.wait_ge(ve.h, ef_done)
        s_.activation(out=v2[:, :], in_=ef1[:, :], func=AF.Exp)
        s_.activation(out=v3[:, :], in_=ef2[:, :], func=AF.Exp, scale=-1.0)
        s_.activation(out=v4[:, :], in_=WF[:, 0:1024], func=AF.Exp).then_inc(se.h, 1)
    se.n += 1
    exps_done = se.n

    @block.vector
    def _(v, exps_done=exps_done):
        v.wait_ge(se.h, exps_done)

        def step(inst):
            ve.inc(inst)
            v.wait_ge(ve.h, ve.n)
        step(v.tensor_tensor(out=v2[:, :], in0=v2[:, :], in1=v0[:, :], op=ALU.mult))
        step(v.tensor_tensor(out=v3[:, :], in0=v3[:, :], in1=sG[:, :], op=ALU.mult))
        step(v.tensor_tensor(out=v4[:, :], in0=v4[:, :], in1=v0[:, :], op=ALU.mult))
        E._sg3_facts = ve.n
        step(v.tensor_tensor(out=sG[:, :], in0=sG[:, :], in1=v0[:, :], op=ALU.mult))
        for dc in range(8):
            step(v.tensor_scalar(out=sG[:, dc * 128:dc * 128 + 128],
                                 in0=sG[:, dc * 128:dc * 128 + 128],
                                 scalar1=sb["pcols"][:, dc, 7:8], scalar2=None, op0=ALU.mult))
    E._sg3_prod = ve.n


def emit_sg4(E, x5_done):
    nc, block, sb, sems = E.nc, E.block, E.sb, E.sems
    te, ve, se = sems["te"], sems["ve"], sems["se"]
    V = E.V
    sB, sC, v1 = V["sB"], V["sC"], V["v1"]
    x5v = sB.rearrange("p (c n) -> p c n", c=8)
    tg1 = sb["cwbc1"][:, 128:256]
    tg2 = sb["cwbc1"][0:32, 256:384]

    @block.tensor
    def _(t, x5_done=x5_done, sev=se.n):
        t.wait_ge(ve.h, x5_done)
        t.wait_ge(se.h, sev)
        inst = None
        for c in range(8):
            inst = t.matmul(E.pb(4, 128, 256), lhsT=sb["wg1sb"][:, c, 0:128],
                            rhs=x5v[:, c, :], start=(c == 0), stop=(c == 7))
        for c in range(8):
            inst = t.matmul(E.pb(4, 256, 384)[0:32, :], lhsT=sb["wg1sb"][:, c, 128:160],
                            rhs=x5v[:, c, :], start=(c == 0), stop=(c == 7))
        inst.then_inc(te.h, 1)
    te.n += 1
    g1mm = te.n

    @block.scalar
    def _(s_, g1mm=g1mm):
        s_.wait_ge(te.h, g1mm)
        s_.activation(out=tg1, in_=E.pb(4, 128, 256), func=AF.Tanh)
        s_.activation(out=tg2, in_=E.pb(4, 256, 384)[0:32, :], func=AF.Tanh).then_inc(se.h, 1)
    se.n += 1
    tg_done = se.n
    E._sg4_tg = tg_done

    @block.tensor
    def _(t, tg_done=tg_done, rv=E._sg3_prod):
        t.wait_ge(se.h, tg_done)
        t.wait_ge(ve.h, rv)
        inst = None
        for dc in range(8):
            inst = t.matmul(_quarter(E, dc), lhsT=sb["wg2a"][:, dc * 128:dc * 128 + 128],
                            rhs=tg1, start=True, stop=False)
            inst = t.matmul(_quarter(E, dc), lhsT=sb["wg2b"][0:32, dc * 128:dc * 128 + 128],
                            rhs=tg2, start=False, stop=True)
        inst.then_inc(te.h, 1)
    te.n += 1
    gmm = te.n

    @block.vector
    def _(v, gmm=gmm):
        v.wait_ge(te.h, gmm)

        def step(inst):
            ve.inc(inst)
            v.wait_ge(ve.h, ve.n)
        for dc in range(8):
            step(v.tensor_copy(v1[:, dc * 128:dc * 128 + 128], _quarter(E, dc)))
        V2 = E.V
        step(v.tensor_scalar(out=sB[:, :], in0=sC[:, :], scalar1=sb["parsb"][:, 1:2],
                             scalar2=None, op0=ALU.mult))
        step(v.tensor_scalar(out=V2["v6"][:, :], in0=sC[:, :], scalar1=sb["parsb"][:, 2:3],
                             scalar2=None, op0=ALU.mult))
        step(v.tensor_scalar(out=V2["v7"][:, :], in0=sC[:, :], scalar1=sb["parsb"][:, 3:4],
                             scalar2=None, op0=ALU.mult))
        step(v.tensor_scalar(out=sC[:, :], in0=sC[:, :], scalar1=sb["parsb"][:, 0:1],
                             scalar2=None, op0=ALU.mult))
    E._sg4_ksplit = ve.n


def emit_headloop(E, t0, s_reset, ecole_done, facts_done, prod_done, ksplit_done):
    nc, block, sb, sems = E.nc, E.block, E.sb, E.sems
    te, ve, se, ge = sems["te"], sems["ve"], sems["se"], sems["ge"]
    V = E.V
    fmh = V["fmh"]
    sB, sC, sD, sG, sH = V["sB"], V["sC"], V["sD"], V["sG"], V["sH"]
    v2, v3, v4 = V["v2"], V["v3"], V["v4"]
    WRK = E.sb["WRK"]

    def ybank(h):
        b = 0 if h < 8 else 1
        yc = ((h % 8) // 2) * 128
        return E.pb(b, yc, yc + 128)[(h % 2) * 64:(h % 2) * 64 + 64, :]

    def mslot(pb_i, h):
        return E.pb(pb_i, (h // 2) * 64, (h // 2) * 64 + 64)[(h % 2) * 64:(h % 2) * 64 + 64, :]

    @block.tensor
    def _(t, vev=ve.n, sev=se.n, prod_done=prod_done, facts_done=facts_done,
          s_reset=s_reset):
        t.wait_ge(ve.h, ksplit_done)
        t.wait_ge(se.h, sev)
        t.wait_ge(ge.h, s_reset)
        t.matmul(E.pb(0), lhsT=sb["zrow"][0:1, :], rhs=sb["onesb"][0:1, 0:512],
                 start=True, stop=False)
        t.matmul(E.pb(1), lhsT=sb["zrow"][0:1, :], rhs=sb["onesb"][0:1, 0:512],
                 start=True, stop=False)
        inst = None
        for h in range(16):
            hr = (h % 2) * 64
            inst = t.matmul(E.pb(4, 384 + h, 385 + h), lhsT=fmh(sG, h),
                            rhs=sb["onesb"][hr:hr + 64, 0:1], start=True, stop=True)
        for h in range(16):
            hr = (h % 2) * 64
            inst = t.matmul(ybank(h), lhsT=sb["Ssb"][hr:hr + 64, h // 2, :],
                            rhs=fmh(v4, h), start=False, stop=False)
        inst.then_inc(te.h, 1)
    te.n += 1
    scol_done = te.n

    m2_done = {}
    for h in range(16):
        ab = 2 if h % 2 == 0 else 3
        aq = ((h // 2) % 4) * 128
        slot = (h % 2) * 512
        AT = WRK[:, slot:slot + 128]
        Dt = WRK[:, slot + 128:slot + 256]

        @block.tensor
        def _(t, h=h, ab=ab, aq=aq, facts_done=facts_done, vprev=m2_done.get(h - 4, 0),
              scol_done=scol_done):
            t.wait_ge(ve.h, facts_done)
            t.wait_ge(ve.h, ksplit_done)
            if h >= 8:
                t.wait_ge(ve.h, m2_done[h - 8][1])
            t.matmul(E.pb(ab, aq, aq + 128), lhsT=fmh(v3, h), rhs=fmh(v2, h),
                     start=True, stop=True).then_inc(te.h, 1)
        te.n += 1
        amm = te.n

        @block.vector
        def _(v, h=h, ab=ab, aq=aq, amm=amm, scol_done=scol_done,
              prev=m2_done.get(h - 2, (0, 0))[0]):
            v.wait_ge(te.h, amm)
            v.wait_ge(te.h, scol_done)
            v.wait_ge(te.h, prev)
            v.tensor_tensor(out=AT, in0=E.pb(ab, aq, aq + 128), in1=sb["maskTsb"][:, :],
                            op=ALU.mult)
            v.tensor_scalar(out=Dt, in0=sb["identsb"][:, :],
                            scalar1=E.pb(4, 384 + h, 385 + h), scalar2=None,
                            op0=ALU.mult).then_inc(ve.h, 1)
        ve.n += 1
        atd = ve.n

        @block.tensor
        def _(t, h=h, atd=atd):
            t.wait_ge(ve.h, atd)
            t.matmul(ybank(h), lhsT=sH[:, 64 * h:64 * h + 64], rhs=AT, start=False, stop=False)
            t.matmul(ybank(h), lhsT=sH[:, 64 * h:64 * h + 64], rhs=Dt,
                     start=False, stop=False).then_inc(te.h, 1)
        te.n += 1
        m2_done[h] = (te.n, atd)

    KQ = [sC, sB, V["v6"], V["v7"]]
    zupd_done = 0
    for p in range(4):
        @block.tensor
        def _(t, p=p, zu=zupd_done, ks=ksplit_done, v2d=E._sg2_done):
            t.wait_ge(ve.h, ks)
            t.wait_ge(ve.h, v2d)
            if p > 0:
                t.wait_ge(ve.h, zu)
            inst = None
            for sub in range(2):
                sg = 2 * p + sub
                rb = 64 * (sg // 4)
                kq = KQ[sg % 4]
                for h in range(16):
                    inst = t.matmul(mslot(5 + sub, h), lhsT=kq[rb:rb + 64, 64 * h:64 * h + 64],
                                    rhs=sH[rb:rb + 64, 64 * h:64 * h + 64],
                                    start=True, stop=True)
            inst.then_inc(te.h, 1)
        te.n += 1
        mpair = te.n

        for sub in range(2):
            sg = 2 * p + sub
            pb_i = 5 if sub == 0 else 6
            if sg > 0:
                @block.tensor
                def _(t, sg=sg, zu=zupd_done):
                    t.wait_ge(ve.h, zu)
                    inst = None
                    for h in range(16):
                        hr = (h % 2) * 64
                        inst = t.matmul(ybank(h)[:, 16 * sg:16 * sg + 16],
                                        lhsT=sb["Zsb"][hr:hr + 64, h // 2, :],
                                        rhs=fmh(v2, h)[:, 16 * sg:16 * sg + 16],
                                        start=False, stop=False)
                    inst.then_inc(te.h, 1)
                te.n += 1
            ymm = te.n

            @block.vector
            def _(v, sg=sg, pb_i=pb_i, mpair=mpair, ymm=ymm, ec=ecole_done):
                v.wait_ge(te.h, mpair)
                v.wait_ge(te.h, ymm)
                v.wait_ge(se.h, ec)
                inst = None
                for c in range(8):
                    msl = E.pb(pb_i, c * 64, c * 64 + 64)
                    if sg == 0:
                        inst = v.tensor_copy(sb["Zsb"][:, c, :], msl)
                    else:
                        inst = v.scalar_tensor_tensor(out=sb["Zsb"][:, c, :],
                                                      in0=sb["Zsb"][:, c, :],
                                                      scalar=sb["ecole"][:, c, sg:sg + 1],
                                                      in1=msl, op0=ALU.mult, op1=ALU.add)
                inst.then_inc(ve.h, 1)
            ve.n += 1
            zupd_done = ve.n

    @block.tensor
    def _(t, zu=zupd_done):
        t.wait_ge(ve.h, zu)
        inst = None
        for h in range(16):
            inst = t.matmul(mslot(5, h), lhsT=sD[:, 64 * h:64 * h + 64],
                            rhs=sH[:, 64 * h:64 * h + 64], start=True, stop=True)
        inst.then_inc(te.h, 1)
    te.n += 1
    mhat = te.n

    @block.vector
    def _(v, mhat=mhat, ec=ecole_done):
        v.wait_ge(te.h, mhat)
        v.wait_ge(se.h, ec)
        inst = None
        for c in range(8):
            inst = v.scalar_tensor_tensor(out=sb["Ssb"][:, c, :], in0=sb["Ssb"][:, c, :],
                                          scalar=sb["ecole"][:, c, 8:9],
                                          in1=E.pb(5, c * 64, c * 64 + 64),
                                          op0=ALU.mult, op1=ALU.add)
        inst.then_inc(ve.h, 1)
    ve.n += 1
    E._hl_supd = ve.n


def emit_gn_wo(E, t0):
    nc, block, sb, sems = E.nc, E.block, E.sb, E.sems
    te, ve, se = sems["te"], sems["ve"], sems["se"]
    V = E.V
    fmh = V["fmh"]
    sE, sF, v1, v5 = V["sE"], V["sF"], V["v1"], V["v5"]
    WF, wo = V["WF"], V["wo"]
    EPS = 1e-5 * NH

    @block.vector
    def _(v, tev=te.n, su=E._hl_supd):
        v.wait_ge(te.h, tev)
        v.wait_ge(ve.h, su)
        v.tensor_copy(v5[:, 0:512], E.pb(0))
        v.tensor_copy(v5[:, 512:1024], E.pb(1)).then_inc(ve.h, 1)
    ve.n += 1
    yevac = ve.n

    @block.scalar
    def _(s_, yevac=yevac):
        s_.wait_ge(ve.h, yevac)
        s_.activation(out=sE[:, :], in_=v5[:, :], func=AF.Square).then_inc(se.h, 1)
    se.n += 1
    ysq_done = se.n

    @block.tensor
    def _(t, ysq_done=ysq_done, yevac=yevac):
        t.wait_ge(se.h, ysq_done)
        t.wait_ge(ve.h, yevac)
        inst = None
        for h in range(16):
            hr = (h % 2) * 64
            inst = t.matmul(E.pb(4, 400 + 2 * h, 401 + 2 * h), lhsT=fmh(v5, h),
                            rhs=sb["onesb"][hr:hr + 64, 0:1], start=True, stop=True)
            inst = t.matmul(E.pb(4, 401 + 2 * h, 402 + 2 * h), lhsT=fmh(sE, h),
                            rhs=sb["onesb"][hr:hr + 64, 0:1], start=True, stop=True)
        inst.then_inc(te.h, 1)
    te.n += 1
    stats_done = te.n

    @block.vector
    def _(v, stats_done=stats_done):
        v.wait_ge(te.h, stats_done)

        def step(inst):
            ve.inc(inst)
            v.wait_ge(ve.h, ve.n)
        sums = E.pb(4, 400, 432).rearrange("p (a b) -> p a b", b=2)
        mc3 = WF[:, 1024:1040].rearrange("p (a b) -> p a b", b=1)
        va3 = WF[:, 1056:1072].rearrange("p (a b) -> p a b", b=1)
        step(v.tensor_scalar(out=mc3, in0=sums[:, :, 0:1], scalar1=1.0 / 64,
                             scalar2=None, op0=ALU.mult))
        step(v.tensor_scalar(out=va3, in0=sums[:, :, 1:2], scalar1=1.0 / 64,
                             scalar2=None, op0=ALU.mult))
        step(v.tensor_tensor(out=WF[:, 1072:1088], in0=WF[:, 1024:1040],
                             in1=WF[:, 1024:1040], op=ALU.mult))
        step(v.scalar_tensor_tensor(out=WF[:, 1056:1072], in0=WF[:, 1072:1088],
                                    scalar=-1.0, in1=WF[:, 1056:1072],
                                    op0=ALU.mult, op1=ALU.add))
        step(v.tensor_scalar(out=WF[:, 1056:1072], in0=WF[:, 1056:1072], scalar1=EPS,
                             scalar2=None, op0=ALU.add))
        quake_rsqrt(v, sems["ve"], WF[:, 1040:1056], WF[:, 1056:1072], WF[:, 1072:1088])
    gn_stats = ve.n

    @block.tensor
    def _(t, gn_stats=gn_stats):
        t.wait_ge(ve.h, gn_stats)
        t.transpose(E.pb(7, 0, 128)[0:16, :], in_=WF[:, 1024:1040],
                    identity=sb["identf"][:, :])
        t.transpose(E.pb(7, 128, 256)[0:16, :], in_=WF[:, 1040:1056],
                    identity=sb["identf"][:, :]).then_inc(te.h, 1)
    te.n += 1
    trn = te.n

    @block.vector
    def _(v, trn=trn):
        v.wait_ge(te.h, trn)
        v.tensor_copy(sb["mrsb"][:, :], E.pb(7, 0, 128)[0:16, :])
        v.tensor_copy(sb["rssb"][:, :], E.pb(7, 128, 256)[0:16, :]).then_inc(ve.h, 1)
    ve.n += 1
    mr_done = ve.n

    @block.tensor
    def _(t, mr_done=mr_done):
        t.wait_ge(ve.h, mr_done)
        inst = None
        for c in range(8):
            inst = t.matmul(_quarter(E, c), lhsT=sb["INDsb"][:, c * 128:c * 128 + 128],
                            rhs=sb["mrsb"][:, :], start=True, stop=True)
            rb = 5 if c < 4 else 6
            inst = t.matmul(E.pb(rb, (c % 4) * 128, (c % 4) * 128 + 128),
                            lhsT=sb["INDsb"][:, c * 128:c * 128 + 128],
                            rhs=sb["rssb"][:, :], start=True, stop=True)
        inst.then_inc(te.h, 1)
    te.n += 1
    bc_done = te.n

    @block.vector
    def _(v, bc_done=bc_done):
        v.wait_ge(te.h, bc_done)

        def step(inst):
            ve.inc(inst)
            v.wait_ge(ve.h, ve.n)
        sFv = sF.rearrange("p (c n) -> p c n", c=8)
        v5v = v5.rearrange("p (c n) -> p c n", c=8)
        v1v = v1.rearrange("p (c n) -> p c n", c=8)
        for c in range(8):
            rb = 5 if c < 4 else 6
            step(v.tensor_tensor(out=sFv[:, c, :], in0=v5v[:, c, :], in1=_quarter(E, c),
                                 op=ALU.subtract))
            step(v.scalar_tensor_tensor(out=sFv[:, c, :], in0=sFv[:, c, :],
                                        scalar=sb["pcols"][:, c, 8:9],
                                        in1=E.pb(rb, (c % 4) * 128, (c % 4) * 128 + 128),
                                        op0=ALU.mult, op1=ALU.mult))
            step(v.scalar_tensor_tensor(out=sFv[:, c, :], in0=sFv[:, c, :],
                                        scalar=sb["pcols"][:, c, 9:10],
                                        in1=v1v[:, c, :], op0=ALU.add, op1=ALU.mult))
    apply_done = ve.n

    @block.tensor
    def _(t, apply_done=apply_done):
        t.wait_ge(ve.h, apply_done)
        sFv = sF.rearrange("p (c n) -> p c n", c=8)
        inst = None
        for dc in range(8):
            rb = 5 if dc < 4 else 6
            for c in range(8):
                inst = t.matmul(E.pb(rb, (dc % 4) * 128, (dc % 4) * 128 + 128),
                                lhsT=wo[:, c, dc * 128:dc * 128 + 128],
                                rhs=sFv[:, c, :], start=(c == 0), stop=(c == 7))
        inst.then_inc(te.h, 1)
    te.n += 1
    womm = te.n

    @block.vector
    def _(v, womm=womm, t0=t0):
        v.wait_ge(te.h, womm)
        inst = None
        for dc in range(8):
            rb = 5 if dc < 4 else 6
            inst = v.tensor_tensor(out=sb["ACC2"][:, dc, t0:t0 + 128],
                                   in0=sb["ACC2"][:, dc, t0:t0 + 128],
                                   in1=E.pb(rb, (dc % 4) * 128, (dc % 4) * 128 + 128),
                                   op=ALU.add)
        inst.then_inc(ve.h, 1)
    ve.n += 1


def emit_finish(E):
    sems = E.sems

    @E.block.sync
    def _(sy):
        for s in ["dma0", "dma1", "dma2"]:
            sy.wait_ge(sems[s].h, sems[s].n)

    @E.block.gpsimd
    def _(g):
        for s in ["te", "ve", "se"]:
            g.wait_ge(sems[s].h, sems[s].n)


def kernel(**inputs):
    I = inputs
    x = np.asarray(I["L0_attn_x"], np.float32)
    nc = build_program()
    shared = _shared_inputs(I)
    in_maps = []
    for c in range(NCORE):
        m = dict(shared)
        xs = x[c * BPC:(c + 1) * BPC].reshape(NTOK, D)
        m["x_fm"] = np.ascontiguousarray(xs.T)
        in_maps.append(m)
    res = run_bass_kernel_spmd(nc, in_maps, core_ids=list(range(NCORE)))
    outs = []
    for c in range(NCORE):
        o = res.results[c]["out_fm"]
        outs.append(np.ascontiguousarray(np.asarray(o).T).reshape(BPC, T, D))
    return np.concatenate(outs, axis=0)


def _shared_inputs(I):
    asf = lambda k: np.asarray(I[k], np.float32)
    m = {
        "wqT": asf("L0_attn_Wq").T, "wkT": asf("L0_attn_Wk").T,
        "wvT": asf("L0_attn_Wv").T, "woT": asf("L0_attn_Wo").T,
        "bq": asf("L0_attn_bq"), "bk": asf("L0_attn_bk"),
        "bv": asf("L0_attn_bv"), "bo": asf("L0_attn_bo"),
        "embKT": asf("L0_attn_embK").T, "embV": asf("L0_attn_embV"),
        "r0_W": asf("L0_ffn_router_W"), "r0_b": asf("L0_ffn_router_b"),
        "sW1": asf("L0_ffn_sW1"), "sb1": asf("L0_ffn_sb1"),
        "sW2": asf("L0_ffn_sW2"), "sb2": asf("L0_ffn_sb2"),
        "eW1": asf("L0_ffn_eW1"), "eb1": asf("L0_ffn_eb1"),
        "eW2": asf("L0_ffn_eW2"), "eb2": asf("L0_ffn_eb2"),
        "eb1T": np.concatenate([asf("L0_ffn_eb1"), asf("L0_ffn_sb1")[None]], 0).T,
        "eb2m": np.concatenate([asf("L0_ffn_eb2"), asf("L0_ffn_sb2")[None]], 0),
        "miu_x": asf("L1_attn_miu_x").reshape(-1), "lam": asf("L1_attn_lam").reshape(5, D),
        "A1": asf("L1_attn_A"), "Bp": asf("L1_attn_Bp"),
        "td_miu": asf("L1_attn_td_miu").reshape(-1), "tdA": asf("L1_attn_tdA"),
        "tdB": asf("L1_attn_tdB"), "u1": asf("L1_attn_u").reshape(-1),
        "l1wkT": asf("L1_attn_Wk").T, "l1wvT": asf("L1_attn_Wv").T,
        "l1wrT": asf("L1_attn_Wr").T, "l1woT": asf("L1_attn_Wo").T,
        "Wg1": asf("L1_attn_Wg1"), "Wg2": asf("L1_attn_Wg2"),
        "gamma": asf("L1_attn_gamma"), "beta": asf("L1_attn_beta"),
        "g1_w": asf("L1_ffn_gate_w"), "g1_b": asf("L1_ffn_gate_b"),
        "fsw1": asf("L1_ffn_sw1"), "fsb1": asf("L1_ffn_sb1"),
        "fsw2": asf("L1_ffn_sw2"), "fsb2": asf("L1_ffn_sb2"),
        "frw1": asf("L1_ffn_rw1"), "frb1": asf("L1_ffn_rb1"),
        "frw2": asf("L1_ffn_rw2"), "frb2": asf("L1_ffn_rb2"),
        "fb1T": np.concatenate([asf("L1_ffn_rb1"), asf("L1_ffn_sb1")], 0).T,
        "fb2m": np.concatenate([asf("L1_ffn_rb2"), asf("L1_ffn_sb2")], 0),
    }
    m = {k: np.ascontiguousarray(v) for k, v in m.items()}
    import ml_dtypes
    m["identity"] = np.eye(128, dtype=ml_dtypes.bfloat16)
    ekt = asf("L0_attn_embK").T
    m["embKT2"] = np.concatenate([ekt, ekt], 0)
    m["qkvo_bias"] = np.stack([asf("L0_attn_bq"), asf("L0_attn_bk"), asf("L0_attn_bv"), asf("L0_attn_bo")], 0)
    kk = np.arange(128)[:, None]; pp = np.arange(128)[None, :]
    m["rmask"] = np.stack([(kk - pp >= 64), (kk - pp >= -64)], 0).astype(ml_dtypes.bfloat16)
    jj = np.arange(127)[None, :]
    m["bwmask"] = np.stack([(pp.T + jj >= 63), (pp.T + jj <= 190)], 0).astype(ml_dtypes.bfloat16)
    ss = np.arange(128)[:, None]; tt = np.arange(128)[None, :]
    m["maskT"] = ((ss // 16 == tt // 16) & (ss < tt)).astype(ml_dtypes.bfloat16)
    m["Mkhat_c"] = (ss > tt).astype(ml_dtypes.bfloat16)
    cn = np.minimum((np.arange(128) // 16 + 1) * 16 + 7, 127)
    m["Mkbrv_c"] = ((tt < ss) & (ss <= cn[None, :])).astype(ml_dtypes.bfloat16)
    ind = np.zeros((16, 1024), np.float32)
    for c in range(8):
        ind[2 * c, c * 128:c * 128 + 64] = 1.0
        ind[2 * c + 1, c * 128 + 64:c * 128 + 128] = 1.0
    m["INDf"] = ind.astype(ml_dtypes.bfloat16)
    tr = np.arange(128)
    m["l1rows"] = np.stack([(127 - tr).astype(np.float32),
                            (cn - tr).astype(np.float32)], 0)
    pr = np.arange(128)
    m["l1cons"] = np.stack([((pr // 16) % 4 == q).astype(np.float32) for q in range(4)], 1)
    return m
